# version 03 (final)
# baseline (speedup 1.0000x reference)
import contextlib
import numpy as np
import ml_dtypes
import concourse.bass as bass
import concourse.mybir as mybir
from concourse.bass_utils import run_bass_kernel_spmd

F32 = mybir.dt.float32
BF16 = mybir.dt.bfloat16
I32 = mybir.dt.int32
AF = mybir.ActivationFunctionType
ALU = mybir.AluOpType
AX = mybir.AxisListType
BF = ml_dtypes.bfloat16

ENGINES = ("sync", "gpsimd", "tensor", "vector", "scalar")


class Prog:

    def __init__(self, name="prog", scratch=16384):
        self.nc = bass.Bass("TRN2", target_bir_lowering=False, dynamic_dma_scratch_size=scratch)
        self.st = contextlib.ExitStack()
        self.q = {e: [] for e in ENGINES}
        self._names = set()

    def _nm(self, base):
        i = 0
        n = base
        while n in self._names:
            i += 1
            n = f"{base}{i}"
        self._names.add(n)
        return n

    def inp(self, name, shape, dtype=F32):
        return self.nc.dram_tensor(name, list(shape), dtype, kind="ExternalInput").ap()

    def outp(self, name, shape, dtype=F32):
        return self.nc.dram_tensor(name, list(shape), dtype, kind="ExternalOutput").ap()

    def dram(self, name, shape, dtype=F32):
        return self.nc.dram_tensor(self._nm(name), list(shape), dtype, kind="Internal").ap()

    def sb(self, name, shape, dtype=F32):
        return self.st.enter_context(self.nc.sbuf_tensor(self._nm("sb_" + name), list(shape), dtype))

    def ps(self, name, shape, dtype=F32):
        return self.st.enter_context(self.nc.psum_tensor(self._nm("ps_" + name), list(shape), dtype))

    def sem(self, name):
        return self.st.enter_context(self.nc.semaphore(self._nm(name)))

    def on(self, eng, fn, wait=(), inc=()):
        self.q[eng].append((fn, list(wait), list(inc)))

    def build(self):
        with self.nc.Block() as blk:
            for ename in ENGINES:
                ops = self.q[ename]
                if not ops:
                    continue

                def mk(ops=ops):
                    def body(eng):
                        for fn, waits, incs in ops:
                            for (s, v) in waits:
                                eng.wait_ge(s, v)
                            inst = fn(eng)
                            for (s, n) in incs:
                                assert inst is not None
                                inst.then_inc(s, n)
                    return body

                getattr(blk, ename)(mk())
        self.st.close()
        return self.nc


class Ctr:

    def __init__(self, prog, name):
        self.s = prog.sem(name)
        self.v = 0

    def inc(self, n=1):
        self.v += n
        return [(self.s, n)]

    def at(self):
        return [(self.s, self.v)]


class DmaCtr:

    def __init__(self, prog, name, nbuf):
        self.ss = [prog.sem(f"{name}{i}") for i in range(nbuf)]
        self.vs = [0] * nbuf
        self.n = nbuf

    def inc(self, i, n=16):
        self.vs[i % self.n] += n
        return [(self.ss[i % self.n], n)]

    def at(self, i):
        return [(self.ss[i % self.n], self.vs[i % self.n])]


_NC_CACHE = {}


def cached_nc(key, builder):
    if key not in _NC_CACHE:
        _NC_CACHE[key] = builder()
    return _NC_CACHE[key]


def runp(nc, in_maps):
    res = run_bass_kernel_spmd(nc, in_maps, core_ids=list(range(len(in_maps))))
    return res.results


def build_ln(T=2048, D=1024):
    p = Prog("ln")
    xd = p.inp("x", [T, D])
    xnd = p.outp("xn", [T, D])
    nt = T // 128
    xt = [p.sb("xt", [128, D]) for _ in range(2)]
    x2 = p.sb("x2", [128, D])
    s1 = p.sb("s1", [128, 1])
    s2 = p.sb("s2", [128, 1])
    mean = p.sb("mean", [128, 1])
    ms = p.sb("ms", [128, 1])
    var = p.sb("var", [128, 1])
    sd = p.sb("sd", [128, 1])
    rstd = p.sb("rstd", [128, 1])
    xnt = [p.sb("xnt", [128, D]) for _ in range(2)]
    ld = Ctr(p, "ld")
    vec = Ctr(p, "vec")
    act = Ctr(p, "act")
    sto = DmaCtr(p, "sto", 2)
    var2 = p.sb("var2", [128, 1])
    for i in range(nt):
        b = i % 2
        w = sto.at(b) if i >= 2 else []
        p.on("sync", lambda e, i=i, b=b: e.dma_start(out=xt[b][:], in_=xd[i * 128:(i + 1) * 128, :]),
             wait=w, inc=ld.inc(16))
        p.on("vector", lambda e, b=b: e.reduce_sum(s1[:], xt[b][:], axis=AX.X),
             wait=ld.at() + vec.at(), inc=vec.inc())
        p.on("scalar", lambda e, b=b: e.activation(x2[:], xt[b][:], AF.Square),
             wait=ld.at() + ([(vec.s, vec.v)] if i > 0 else []), inc=act.inc())
        p.on("vector", lambda e: e.reduce_sum(s2[:], x2[:], axis=AX.X),
             wait=act.at() + vec.at(), inc=vec.inc())
        p.on("vector", lambda e: e.tensor_scalar(mean[:], s1[:], 1.0 / D, None, ALU.mult),
             wait=vec.at(), inc=vec.inc())
        p.on("vector", lambda e: e.tensor_mul(ms[:], mean[:], mean[:]), wait=vec.at(), inc=vec.inc())
        p.on("vector", lambda e: e.scalar_tensor_tensor(var[:], s2[:], 1.0 / D, ms[:], ALU.mult, ALU.subtract),
             wait=vec.at(), inc=vec.inc())
        p.on("vector", lambda e: e.tensor_scalar(var2[:], var[:], 1e-5, None, ALU.add),
             wait=vec.at(), inc=vec.inc())
        p.on("scalar", lambda e: e.activation(sd[:], var2[:], AF.Sqrt), wait=vec.at(), inc=act.inc())
        p.on("vector", lambda e: e.reciprocal(rstd[:], sd[:]), wait=act.at() + vec.at(), inc=vec.inc())
        p.on("vector", lambda e, b=b: e.tensor_scalar(xnt[b][:], xt[b][:], mean[:], rstd[:],
                                                      ALU.subtract, ALU.mult),
             wait=vec.at(), inc=vec.inc())
        p.on("sync", lambda e, i=i, b=b: e.dma_start(out=xnd[i * 128:(i + 1) * 128, :], in_=xnt[b][:]),
             wait=vec.at(), inc=sto.inc(b))
    p.on("sync", lambda e: e.nop(), wait=sto.at(0) + sto.at(1))
    return p.build()


def build_proj(T, D, outs, in_dtype=F32, name="proj"):
    p = Prog(name)
    xnd = p.inp("xnT", [D, T], in_dtype)
    nd = D // 128
    xsb = p.sb("xnT_sb", [128, nd, T], BF16)
    ld = Ctr(p, "ld")
    ldg = Ctr(p, "ldg")
    ini = Ctr(p, "ini")
    if in_dtype == F32:
        for c in range(nd):
            p.on("gpsimd", lambda e, c=c: e.dma_start(out=xsb[:, c, :], in_=xnd[c * 128:(c + 1) * 128, :]),
                 inc=ldg.inc(16))
    else:
        for c in range(nd):
            p.on("sync", lambda e, c=c: e.dma_start(out=xsb[:, c, :], in_=xnd[c * 128:(c + 1) * 128, :]),
                 inc=ld.inc(16))

    mm = Ctr(p, "mm")
    act = Ctr(p, "act")
    sto = DmaCtr(p, "sto", 2)
    onesb = None
    specs = []
    for o in outs:
        cols = o["cols"]
        wdt = o.get("wdt", F32)
        wd = p.inp("W_" + o["name"], [D, cols], wdt)
        wsb = p.sb("w_" + o["name"], [128, nd, cols], BF16)
        weng = "gpsimd" if wdt == F32 else "sync"
        wctr = ldg if wdt == F32 else ld
        for c in range(nd):
            p.on(weng, lambda e, c=c, wd=wd, wsb=wsb: e.dma_start(out=wsb[:, c, :], in_=wd[c * 128:(c + 1) * 128, :]),
                 inc=wctr.inc(16))
        bsb = None
        if o.get("bias"):
            if o["layout"] == "fT":
                bd = p.inp("b_" + o["name"], [128, cols // 128])
                bsb = p.sb("b_" + o["name"], [128, cols // 128])
                p.on("sync", lambda e, bd=bd, bsb=bsb: e.dma_start(out=bsb[:], in_=bd[:]), inc=ld.inc(16))
            else:
                bd = p.inp("b_" + o["name"], [1, cols])
                bsb = p.sb("b_" + o["name"], [1, cols], BF16)
                p.on("gpsimd", lambda e, bd=bd, bsb=bsb: e.dma_start(out=bsb[:], in_=bd[:]), inc=ldg.inc(16))
                if onesb is None:
                    onesb = p.sb("ones1", [1, 128], BF16)
                    p.on("vector", lambda e: e.memset(onesb[:], 1.0), inc=ini.inc(1))
        od = p.outp("o_" + o["name"], [cols, T] if o["layout"] == "fT" else [T, cols],
                    o.get("odt", BF16))
        specs.append((o, wsb, bsb, od))

    ldw_all = [(ld.s, ld.v), (ldg.s, ldg.v)] + ([(ini.s, ini.v)] if ini.v else [])
    psum = [p.ps("ps", [128, 512]) for _ in range(2)]
    stg = [p.sb("stg", [128, 512], BF16) for _ in range(2)]
    stgf = [p.sb("stgf", [128, 512]) for _ in range(2)]
    kstage = [0]

    def emit_out(o, wsb, bsb, od):
        cols = o["cols"]
        fT = o["layout"] == "fT"
        func = o.get("act") or AF.Copy
        odt_f32 = o.get("odt") == F32
        ncol = min(cols, 512)
        MM = max(cols // 128, 1) if fT else T // 128
        NN = (T // 512) if fT else max(cols // 512, 1)
        for m in range(MM):
            for n in range(NN):
                k = kstage[0]
                kstage[0] += 1
                ps = psum[k % 2]
                sg = (stgf if odt_f32 else stg)[k % 2]
                tok_bias = (not fT) and bsb is not None
                for c in range(nd):
                    if fT:
                        mw = min(cols, 128)
                        def f(e, c=c, m=m, n=n, ps=ps, wsb=wsb, tb=tok_bias, mw=mw):
                            return e.matmul(ps[0:mw, :], wsb[:, c, m * 128:m * 128 + mw],
                                            xsb[:, c, n * 512:(n + 1) * 512],
                                            start=(c == 0), stop=(c == nd - 1 and not tb))
                    else:
                        def f(e, c=c, m=m, n=n, ps=ps, wsb=wsb, tb=tok_bias, ncol=ncol):
                            return e.matmul(ps[:, 0:ncol], xsb[:, c, m * 128:(m + 1) * 128],
                                            wsb[:, c, n * 512:n * 512 + ncol],
                                            start=(c == 0), stop=(c == nd - 1 and not tb))
                    w = []
                    if c == 0:
                        w = ldw_all if k < 2 else [(act.s, k - 1)]
                    p.on("tensor", f, wait=w,
                         inc=(mm.inc() if (c == nd - 1 and not tok_bias) else []))
                if tok_bias:
                    def fb(e, n=n, ps=ps, bsb=bsb, ncol=ncol):
                        return e.matmul(ps[:, 0:ncol], onesb[:, :], bsb[:, n * 512:n * 512 + ncol],
                                        start=False, stop=True)
                    p.on("tensor", fb, inc=mm.inc())
                we = mm.at() + (sto.at(k) if k >= 2 else [])
                scale = o.get("scale")
                if fT:
                    oslc = (slice(m * 128, m * 128 + min(cols, 128)), slice(n * 512, (n + 1) * 512))
                    pslc = ps[0:min(cols, 128), :]
                    gslc = sg[0:min(cols, 128), :]
                else:
                    oslc = (slice(m * 128, (m + 1) * 128), slice(n * 512, n * 512 + ncol))
                    pslc = ps[:, 0:ncol]
                    gslc = sg[:, 0:ncol]
                if scale is not None:
                    assert o.get("act") is None
                    if fT and bsb is not None:
                        def fe(e, pslc=pslc, gslc=gslc, m=m, bsb=bsb, scale=scale):
                            return e.tensor_scalar(gslc, pslc, bsb[:, m:m + 1], scale, ALU.add, ALU.mult)
                    else:
                        def fe(e, pslc=pslc, gslc=gslc, scale=scale):
                            return e.tensor_scalar(gslc, pslc, scale, None, ALU.mult)
                    p.on("vector", fe, wait=we, inc=act.inc())
                elif fT and bsb is not None:
                    if o.get("act") is None:
                        def fe(e, pslc=pslc, gslc=gslc, m=m, bsb=bsb):
                            return e.tensor_scalar(gslc, pslc, bsb[:, m:m + 1], None, ALU.add)
                        p.on("vector", fe, wait=we, inc=act.inc())
                    else:
                        def fe(e, pslc=pslc, gslc=gslc, m=m, bsb=bsb, func=func):
                            return e.activation(gslc, pslc, func, bias=bsb[:, m:m + 1])
                        p.on("scalar", fe, wait=we, inc=act.inc())
                else:
                    def fe(e, pslc=pslc, gslc=gslc, func=func):
                        return e.activation(gslc, pslc, func)
                    p.on("scalar", fe, wait=we, inc=act.inc())
                def fs(e, gslc=gslc, od=od, oslc=oslc):
                    return e.dma_start(out=od[oslc[0], oslc[1]], in_=gslc)
                p.on("sync", fs, wait=act.at(), inc=sto.inc(k))

    for (o, wsb, bsb, od) in specs:
        emit_out(o, wsb, bsb, od)
    p.on("sync", lambda e: e.nop(), wait=sto.at(0) + sto.at(1))
    return p.build()


def own_rows(a, f):
    blocks = a.reshape(32, 128, *a.shape[1:])
    return blocks[f::2].reshape(2048, *a.shape[1:])


def scatter_rows(full, f, part):
    blocks = full.reshape(32, 128, *full.shape[1:])
    blocks[f::2] = part.reshape(16, 128, *full.shape[1:])


def interleave_halves(a0, a1):
    out = np.empty((4096,) + a0.shape[1:], dtype=a0.dtype)
    scatter_rows(out, 0, a0)
    scatter_rows(out, 1, a1)
    return out


def interleave_cols(a0, a1):
    b0 = a0.reshape(a0.shape[0], 16, 128)
    b1 = a1.reshape(a1.shape[0], 16, 128)
    out = np.empty((a0.shape[0], 32, 128), dtype=a0.dtype)
    out[:, 0::2] = b0
    out[:, 1::2] = b1
    return out.reshape(a0.shape[0], 4096)




def attn_masks(f):
    tri = np.where(np.arange(128)[:, None] > np.arange(128)[None, :], -1e9, 0.0).astype(np.float32)
    full = np.full((128, 128), -1e9, np.float32)
    zero = np.zeros((128, 128), np.float32)
    out = np.empty((8, 128, 128), np.float32)
    for r in range(8):
        if f == 0:
            out[r] = tri if r % 2 == 0 else full
        else:
            out[r] = zero if r % 2 == 0 else tri
    return out


OMIN = [0, 0, 1, 1, 2, 2, 3, 3]


def build_attn(H, HD, proj, name="attn"):
    S, T, D = 4096, 2048, 1024
    p = Prog(name, scratch=4096)
    qd = p.inp("qT", [D, T], BF16)
    kd = p.inp("kT", [D, S], BF16)
    vcols = H * (HD + 1) if H > 1 else D
    vd = p.inp("vv", [S, vcols], BF16)
    md = p.inp("masks", [8, 128, 128])
    idnd = p.inp("idn", [128, 128])
    sd_ = p.outp("sT", [D, T], BF16)
    if proj:
        wpd = p.inp("WpT", [D, D])
        bpd = p.inp("bp", [128, 8])

    qsb = p.sb("q", [128, 8, T], BF16)
    ksb = p.sb("k", [128, 8, S], BF16)
    vsb = p.sb("v", [128, 32, vcols], BF16)
    msb = p.sb("m", [128, 8, 128], BF16)
    idnb = p.sb("idn", [128, 128], BF16)
    ones1 = p.sb("ones1", [1, 128], BF16)
    onescol = p.sb("onescol", [128, 1], BF16) if H == 1 else None
    rdenf = p.sb("rdenf", [1, 512])
    rden = p.sb("rden", [1, 512], BF16)
    dbsb = p.sb("db", [128, 512], BF16)

    ld = Ctr(p, "ld")
    ldg = Ctr(p, "ldg")
    ini = Ctr(p, "ini")
    mm = Ctr(p, "mm")
    act = Ctr(p, "act")
    vecc = Ctr(p, "vecc")
    sto = DmaCtr(p, "sto", 2)

    for c in range(8):
        p.on("sync", lambda e, c=c: e.dma_start(out=qsb[:, c, :], in_=qd[c * 128:(c + 1) * 128, :]), inc=ld.inc(16))
        p.on("sync", lambda e, c=c: e.dma_start(out=ksb[:, c, :], in_=kd[c * 128:(c + 1) * 128, :]), inc=ld.inc(16))
    vre = vd.rearrange("(n p) c -> p n c", p=128)
    for g in range(4):
        p.on("sync", lambda e, g=g: e.dma_start(out=vsb[:, 8 * g:8 * (g + 1), :], in_=vre[:, 8 * g:8 * (g + 1), :]), inc=ld.inc(16))
    for r in range(8):
        p.on("gpsimd", lambda e, r=r: e.dma_start(out=msb[:, r, :], in_=md[r]), inc=ldg.inc(16))
    p.on("gpsimd", lambda e: e.dma_start(out=idnb[:], in_=idnd[:]), inc=ldg.inc(16))
    p.on("vector", lambda e: e.memset(ones1[:], 1.0), inc=ini.inc(1))
    if onescol is not None:
        p.on("vector", lambda e: e.memset(onescol[:], 1.0), inc=ini.inc(1))
    if proj:
        osb = p.sb("oT", [128, 8, T], BF16)
        wpsb = p.sb("wp", [128, 8, D], BF16)
        bpsb = p.sb("bp", [128, 8])
        for c in range(8):
            p.on("gpsimd", lambda e, c=c: e.dma_start(out=wpsb[:, c, :], in_=wpd[c * 128:(c + 1) * 128, :]), inc=ldg.inc(16))
        p.on("sync", lambda e: e.dma_start(out=bpsb[:], in_=bpd[:]), inc=ld.inc(16))
    ldall_w = [(ld.s, ld.v), (ldg.s, ldg.v), (ini.s, ini.v)]

    psA = p.ps("psA", [128, 512])
    psB = p.ps("psB", [128, 512])

    if H > 1:
        esb = p.sb("e", [128, 2, 512], BF16)
        pso = p.ps("pso", [65, 512])
        psdb = p.ps("psdb", [64, 512])
        first = True
        for h in range(H):
            hc, ho = h // 2, 64 * (h % 2)
            for u in range(4):
                nkk = 8 * u + 8
                for kk in range(nkk):
                    ps = (psA, psB)[kk % 2]
                    r = kk - 8 * u
                    co = OMIN[r] * 128 if r >= 0 else 0

                    def fsc(e, hc=hc, ho=ho, u=u, kk=kk, ps=ps, co=co, r=r):
                        return e.matmul(ps[:, co:512], ksb[ho:ho + 64, hc, kk * 128:(kk + 1) * 128],
                                        qsb[ho:ho + 64, hc, u * 512 + co:(u + 1) * 512],
                                        start=True, stop=(r < 0))
                    p.on("tensor", fsc, wait=(ldall_w if first else act.at()),
                         inc=(mm.inc() if r < 0 else []))
                    first = False
                    if r >= 0:
                        def fmk(e, ps=ps, r=r, co=co):
                            return e.matmul(ps[:, co:co + 128], idnb[:], msb[:, r, :], start=False, stop=True)
                        p.on("tensor", fmk, inc=mm.inc())

                    def fex(e, ps=ps, kk=kk, co=co):
                        return e.activation(esb[:, kk % 2, co:512], ps[:, co:512], AF.Exp)
                    p.on("scalar", fex, wait=mm.at(), inc=act.inc())

                    def fav(e, h=h, kk=kk, co=co, nkk=nkk):
                        return e.matmul(pso[:, co:512], vsb[:, kk, 65 * h:65 * h + 65],
                                        esb[:, kk % 2, co:512], start=(kk == 0), stop=(kk == nkk - 1))
                    p.on("tensor", fav, wait=act.at() + (vecc.at() if kk == 0 else []),
                         inc=(mm.inc() if kk == nkk - 1 else []))
                p.on("vector", lambda e: e.reciprocal(rdenf[:], pso[64:65, :]), wait=mm.at() + vecc.at(), inc=vecc.inc())
                p.on("vector", lambda e: e.tensor_copy(rden[:], rdenf[:]), wait=vecc.at(), inc=vecc.inc())
                p.on("tensor", lambda e: e.matmul(psdb[:], ones1[:, 0:64], rden[:], start=True, stop=True),
                     wait=vecc.at() + act.at(), inc=mm.inc())
                p.on("scalar", lambda e: e.activation(dbsb[0:64, :], psdb[:], AF.Copy),
                     wait=mm.at() + vecc.at(), inc=act.inc())

                def fnorm(e, hc=hc, ho=ho, u=u):
                    return e.scalar_tensor_tensor(osb[ho:ho + 64, hc, u * 512:(u + 1) * 512],
                                                  pso[0:64, :], 1.0, dbsb[0:64, :], ALU.mult, ALU.mult)
                p.on("vector", fnorm, wait=act.at(), inc=vecc.inc())
        psp = p.ps("psp", [128, 512])
        stg = [p.sb("stg", [128, 512], BF16) for _ in range(2)]
        k = 0
        for m2 in range(8):
            for n in range(4):
                for c in range(8):
                    def fp(e, c=c, m2=m2, n=n):
                        return e.matmul(psp[:], wpsb[:, c, m2 * 128:(m2 + 1) * 128],
                                        osb[:, c, n * 512:(n + 1) * 512],
                                        start=(c == 0), stop=(c == 7))
                    w = []
                    if c == 0:
                        w = vecc.at() if k == 0 else act.at()
                    p.on("tensor", fp, wait=w, inc=(mm.inc() if c == 7 else []))
                we = mm.at() + (sto.at(k) if k >= 2 else [])

                def fe(e, k=k, m2=m2):
                    return e.tensor_scalar(stg[k % 2][:], psp[:], bpsb[:, m2:m2 + 1], None, ALU.add)
                p.on("vector", fe, wait=we, inc=act.inc())

                def fs(e, k=k, m2=m2, n=n):
                    return e.dma_start(out=sd_[m2 * 128:(m2 + 1) * 128, n * 512:(n + 1) * 512], in_=stg[k % 2][:])
                p.on("sync", fs, wait=act.at(), inc=sto.inc(k))
                k += 1
        p.on("sync", lambda e: e.nop(), wait=sto.at(0) + sto.at(1))
    else:
        esb = p.sb("e", [128, 32, 512], BF16)
        pso = p.ps("pso", [128, 512])
        psden = p.ps("psden", [1, 512])
        psdb = p.ps("psdb", [128, 512])
        stg2 = [p.sb("stg2", [128, 512], BF16) for _ in range(2)]
        k = 0
        first = True
        for u in range(4):
            nkk = 8 * u + 8
            for kk in range(nkk):
                ps = (psA, psB)[kk % 2]
                r = kk - 8 * u
                co = OMIN[r] * 128 if r >= 0 else 0
                for c in range(8):
                    def fsc(e, c=c, u=u, kk=kk, ps=ps, co=co, r=r):
                        return e.matmul(ps[:, co:512], ksb[:, c, kk * 128:(kk + 1) * 128],
                                        qsb[:, c, u * 512 + co:(u + 1) * 512],
                                        start=(c == 0), stop=(c == 7 and r < 0))
                    p.on("tensor", fsc,
                         wait=((ldall_w if first else act.at()) if c == 0 else []),
                         inc=(mm.inc() if (c == 7 and r < 0) else []))
                    first = False
                if r >= 0:
                    def fmk(e, ps=ps, r=r, co=co):
                        return e.matmul(ps[:, co:co + 128], idnb[:], msb[:, r, :], start=False, stop=True)
                    p.on("tensor", fmk, inc=mm.inc())

                def fex(e, ps=ps, kk=kk, co=co):
                    return e.activation(esb[:, kk, co:512], ps[:, co:512], AF.Exp)
                p.on("scalar", fex, wait=mm.at(), inc=act.inc())
            for kk in range(nkk):
                r = kk - 8 * u
                co = OMIN[r] * 128 if r >= 0 else 0

                def fden(e, kk=kk, co=co, nkk=nkk):
                    return e.matmul(psden[:, co:512], onescol[:], esb[:, kk, co:512],
                                    start=(kk == 0), stop=(kk == nkk - 1))
                p.on("tensor", fden, wait=((act.at() + vecc.at()) if kk == 0 else []),
                     inc=(mm.inc() if kk == nkk - 1 else []))
            p.on("vector", lambda e: e.reciprocal(rdenf[:], psden[:]), wait=mm.at(), inc=vecc.inc())
            p.on("vector", lambda e: e.tensor_copy(rden[:], rdenf[:]), wait=vecc.at(), inc=vecc.inc())
            p.on("tensor", lambda e: e.matmul(psdb[:], ones1[:], rden[:], start=True, stop=True),
                 wait=vecc.at() + act.at(), inc=mm.inc())
            p.on("scalar", lambda e: e.activation(dbsb[:], psdb[:], AF.Copy),
                 wait=mm.at() + vecc.at(), inc=act.inc())
            for dc in range(8):
                for kk in range(nkk):
                    r = kk - 8 * u
                    co = OMIN[r] * 128 if r >= 0 else 0

                    def fav(e, dc=dc, kk=kk, co=co, nkk=nkk):
                        return e.matmul(pso[:, co:512], vsb[:, kk, dc * 128:(dc + 1) * 128],
                                        esb[:, kk, co:512], start=(kk == 0), stop=(kk == nkk - 1))
                    p.on("tensor", fav, wait=(vecc.at() if kk == 0 else []),
                         inc=(mm.inc() if kk == nkk - 1 else []))

                def fnorm(e, dc=dc, u=u, k=k):
                    return e.scalar_tensor_tensor(stg2[k % 2][:], pso[:], 1.0, dbsb[:], ALU.mult, ALU.mult)
                p.on("vector", fnorm, wait=mm.at() + act.at() + (sto.at(k) if k >= 2 else []), inc=vecc.inc())

                def fs(e, dc=dc, u=u, k=k):
                    return e.dma_start(out=sd_[dc * 128:(dc + 1) * 128, u * 512:(u + 1) * 512], in_=stg2[k % 2][:])
                p.on("sync", fs, wait=vecc.at(), inc=sto.inc(k))
                k += 1
        p.on("sync", lambda e: e.nop(), wait=sto.at(0) + sto.at(1))
    return p.build()


def make_quant_scratch(p, cw):
    return dict(
        absw=p.sb("q_absw", [128, cw]),
        a=p.sb("q_a", [128, cw]),
        t1=p.sb("q_t1", [128, cw]),
        t2=p.sb("q_t2", [128, cw]),
        t3=p.sb("q_t3", [128, cw]),
        t4=p.sb("q_t4", [128, cw]),
        sgn=p.sb("q_sgn", [128, cw]),
        amax=p.sb("q_amax", [128, cw // 16]),
        sc=p.sb("q_sc", [128, cw // 16]),
        sc2=p.sb("q_sc2", [128, cw // 16]),
        sc3=p.sb("q_sc3", [128, cw // 16]),
        rsc=p.sb("q_rsc", [128, cw // 16]),
        hs=p.sb("q_hs", [128, cw // 16]),
        cw=cw,
    )


def emit_quant(p, sc, wt, out, variant, vec, act, consts=None, wait0=()):
    cw = sc["cw"]
    nb = cw // 16
    CMP = ALU.is_gt if variant == "l0" else ALU.is_ge
    w0 = list(wait0)
    p.on("scalar", lambda e: e.activation(sc["absw"][:], wt, AF.Abs), wait=w0 + act.at(), inc=act.inc())
    p.on("scalar", lambda e: e.activation(sc["sgn"][:], wt, AF.Sign), wait=w0 + act.at(), inc=act.inc())
    p.on("vector", lambda e: e.reduce_max(sc["amax"][:], wt.rearrange("p (n b) -> p n b", b=16),
                                          axis=AX.X, apply_absolute_value=True),
         wait=w0 + vec.at(), inc=vec.inc())
    if variant == "l0":
        p.on("vector", lambda e: e.tensor_scalar(sc["sc"][:], sc["amax"][:], 1.0 / 6.0, 1e-12, ALU.mult, ALU.max),
             wait=vec.at(), inc=vec.inc())
        p.on("vector", lambda e: e.tensor_scalar(sc["hs"][:], sc["sc"][:], 0.5, None, ALU.mult),
             wait=vec.at(), inc=vec.inc())
        p.on("vector", lambda e: e.reciprocal(sc["rsc"][:], sc["sc"][:]), wait=vec.at(), inc=vec.inc())
    else:
        gs6, gs, rgs = consts["gs6"], consts["gs"], consts["rgs"]
        p.on("vector", lambda e: e.tensor_scalar(sc["sc"][:], sc["amax"][:], gs6[:], 448.0, ALU.mult, ALU.min),
             wait=vec.at(), inc=vec.inc())
        p.on("vector", lambda e: e.tensor_scalar(sc["sc2"][:].bitcast(I32), sc["sc"][:].bitcast(I32), 20, 1,
                                                 ALU.logical_shift_right, ALU.bitwise_and),
             wait=vec.at(), inc=vec.inc())
        p.on("vector", lambda e: e.scalar_tensor_tensor(sc["sc3"][:].bitcast(I32), sc["sc2"][:].bitcast(I32),
                                                        0x7FFFF, sc["sc"][:].bitcast(I32), ALU.add, ALU.add),
             wait=vec.at(), inc=vec.inc())
        p.on("vector", lambda e: e.tensor_scalar(sc["sc"][:].bitcast(I32), sc["sc3"][:].bitcast(I32), -1048576, None,
                                                 ALU.bitwise_and),
             wait=vec.at(), inc=vec.inc())
        p.on("vector", lambda e: e.tensor_scalar(sc["sc2"][:], sc["sc"][:], 1e-30, None, ALU.max),
             wait=vec.at(), inc=vec.inc())
        p.on("vector", lambda e: e.reciprocal(sc["sc3"][:], sc["sc2"][:]), wait=vec.at(), inc=vec.inc())
        p.on("vector", lambda e: e.tensor_scalar(sc["rsc"][:], sc["sc3"][:], gs[:], None, ALU.mult),
             wait=vec.at(), inc=vec.inc())
        p.on("vector", lambda e: e.tensor_scalar(sc["hs"][:], sc["sc"][:], rgs[:], 0.5, ALU.mult, ALU.mult),
             wait=vec.at(), inc=vec.inc())
    rscb = sc["rsc"][:].unsqueeze(2).broadcast_to([128, nb, 16])
    hsb = sc["hs"][:].unsqueeze(2).broadcast_to([128, nb, 16])
    av = sc["a"][:].rearrange("p (n b) -> p n b", b=16)
    p.on("vector", lambda e: e.tensor_tensor(av, sc["absw"][:].rearrange("p (n b) -> p n b", b=16), rscb, ALU.mult),
         wait=vec.at() + act.at(), inc=vec.inc())
    a = sc["a"][:]
    p.on("vector", lambda e: e.tensor_scalar(sc["t1"][:], a, 0.25, None, CMP), wait=vec.at(), inc=vec.inc())
    p.on("vector", lambda e: e.scalar_tensor_tensor(sc["t2"][:], a, 0.75, sc["t1"][:], CMP, ALU.add),
         wait=vec.at(), inc=vec.inc())
    p.on("vector", lambda e: e.scalar_tensor_tensor(sc["t1"][:], a, 1.25, sc["t2"][:], CMP, ALU.add),
         wait=vec.at(), inc=vec.inc())
    p.on("vector", lambda e: e.scalar_tensor_tensor(sc["t2"][:], a, 1.75, sc["t1"][:], CMP, ALU.add),
         wait=vec.at(), inc=vec.inc())
    p.on("vector", lambda e: e.tensor_scalar(sc["t3"][:], a, 2.5, None, CMP), wait=vec.at(), inc=vec.inc())
    p.on("vector", lambda e: e.scalar_tensor_tensor(sc["t4"][:], a, 3.5, sc["t3"][:], CMP, ALU.add),
         wait=vec.at(), inc=vec.inc())
    p.on("vector", lambda e: e.tensor_scalar(sc["t3"][:], a, 5.0, 4.0, CMP, ALU.mult), wait=vec.at(), inc=vec.inc())
    p.on("vector", lambda e: e.scalar_tensor_tensor(sc["t4"][:], sc["t4"][:], 2.0, sc["t2"][:], ALU.mult, ALU.add),
         wait=vec.at(), inc=vec.inc())
    p.on("vector", lambda e: e.tensor_add(sc["t1"][:], sc["t4"][:], sc["t3"][:]), wait=vec.at(), inc=vec.inc())
    mv = sc["t2"][:].rearrange("p (n b) -> p n b", b=16)
    p.on("vector", lambda e: e.tensor_tensor(mv, sc["sgn"][:].rearrange("p (n b) -> p n b", b=16), hsb, ALU.mult),
         wait=vec.at() + act.at(), inc=vec.inc())
    p.on("vector", lambda e: e.tensor_tensor(out, sc["t1"][:], sc["t2"][:], ALU.mult),
         wait=vec.at(), inc=vec.inc())


def build_quant(R, C, variant, cw=512, name="quant"):
    p = Prog(name)
    wd = p.inp("w", [R, C])
    od = p.outp("wq", [R, C], BF16)
    consts = None
    ld = Ctr(p, "ld")
    vec = Ctr(p, "vec")
    act = Ctr(p, "act")
    sto = DmaCtr(p, "sto", 2)
    if variant == "l2":
        gsd = p.inp("gs", [128, 1])
        gsb = p.sb("gs", [128, 1])
        gs6 = p.sb("gs6", [128, 1])
        rgs = p.sb("rgs", [128, 1])
        p.on("sync", lambda e: e.dma_start(out=gsb[:], in_=gsd[:]), inc=ld.inc(16))
        p.on("vector", lambda e: e.tensor_scalar(gs6[:], gsb[:], 1.0 / 6.0, None, ALU.mult),
             wait=ld.at(), inc=vec.inc())
        p.on("vector", lambda e: e.reciprocal(rgs[:], gsb[:]), wait=ld.at() + vec.at(), inc=vec.inc())
        consts = dict(gs6=gs6, gs=gsb, rgs=rgs)
    sc = make_quant_scratch(p, cw)
    wt = [p.sb("wt", [128, cw]) for _ in range(2)]
    oq = [p.sb("oq", [128, cw], BF16) for _ in range(2)]
    k = 0
    for ri in range(R // 128):
        for ci in range(C // cw):
            b = k % 2
            wld = (vec.at() + act.at()) if k >= 2 else []
            p.on("sync", lambda e, ri=ri, ci=ci, b=b: e.dma_start(
                out=wt[b][:], in_=wd[ri * 128:(ri + 1) * 128, ci * cw:(ci + 1) * cw]),
                wait=wld, inc=ld.inc(16))
            emit_quant(p, sc, wt[b][:], oq[b][:], variant, vec, act, consts,
                       wait0=ld.at() + (sto.at(b) if k >= 2 else []))
            p.on("sync", lambda e, ri=ri, ci=ci, b=b: e.dma_start(
                out=od[ri * 128:(ri + 1) * 128, ci * cw:(ci + 1) * cw], in_=oq[b][:]),
                wait=vec.at(), inc=sto.inc(b))
            k += 1
    p.on("sync", lambda e: e.nop(), wait=sto.at(0) + sto.at(1))
    return p.build()


def build_ffn_gu(I, variant, T=2048, name="ffngu"):
    cwg = 512 if I % 512 == 0 else 256
    p = Prog(name)
    xd = p.inp("xqT", [1024, T], BF16)
    wgd = p.inp("wgT", [1024, I], BF16)
    wud = p.inp("wuT", [1024, I], BF16)
    od = p.outp("hq", [T, I], BF16)
    ld = Ctr(p, "ld")
    ldg = Ctr(p, "ldg")
    mm = Ctr(p, "mm")
    vec = Ctr(p, "vec")
    act = Ctr(p, "act")
    sto = DmaCtr(p, "sto", 2)
    xsb = p.sb("x", [128, 8, T], BF16)
    wgsb = p.sb("wg", [128, 8, I], BF16)
    wusb = p.sb("wu", [128, 8, I], BF16)
    for c in range(8):
        p.on("sync", lambda e, c=c: e.dma_start(out=xsb[:, c, :], in_=xd[c * 128:(c + 1) * 128, :]), inc=ld.inc(16))
        p.on("sync", lambda e, c=c: e.dma_start(out=wgsb[:, c, :], in_=wgd[c * 128:(c + 1) * 128, :]), inc=ld.inc(16))
        p.on("sync", lambda e, c=c: e.dma_start(out=wusb[:, c, :], in_=wud[c * 128:(c + 1) * 128, :]), inc=ld.inc(16))
    consts = None
    if variant == "l2":
        gsd = p.inp("gs", [128, 1])
        gsb = p.sb("gs", [128, 1])
        gs6 = p.sb("gs6", [128, 1])
        rgs = p.sb("rgs", [128, 1])
        p.on("sync", lambda e: e.dma_start(out=gsb[:], in_=gsd[:]), inc=ld.inc(16))
        p.on("vector", lambda e: e.tensor_scalar(gs6[:], gsb[:], 1.0 / 6.0, None, ALU.mult),
             wait=ld.at(), inc=vec.inc())
        p.on("vector", lambda e: e.reciprocal(rgs[:], gsb[:]), wait=ld.at() + vec.at(), inc=vec.inc())
        consts = dict(gs6=gs6, gs=gsb, rgs=rgs)
    ldall = ld.v
    psg = p.ps("psg", [128, 512])
    psu = p.ps("psu", [128, 512])
    sgt = p.sb("sgt", [128, cwg])
    ht = [p.sb("ht", [128, cwg]) for _ in range(2)]
    hq = [p.sb("hq", [128, cwg], BF16) for _ in range(2)]
    sc = make_quant_scratch(p, cwg)
    k = 0
    for m in range(T // 128):
        for n in range(I // cwg):
            b = k % 2
            for c in range(8):
                def fg(e, c=c, m=m, n=n):
                    return e.matmul(psg[:, 0:cwg], xsb[:, c, m * 128:(m + 1) * 128],
                                    wgsb[:, c, n * cwg:(n + 1) * cwg], start=(c == 0), stop=(c == 7))
                def fu(e, c=c, m=m, n=n):
                    return e.matmul(psu[:, 0:cwg], xsb[:, c, m * 128:(m + 1) * 128],
                                    wusb[:, c, n * cwg:(n + 1) * cwg], start=(c == 0), stop=(c == 7))
                w = []
                if c == 0:
                    w = [(ld.s, ldall)] if k == 0 else vec.at()
                p.on("tensor", fg, wait=w, inc=(mm.inc() if c == 7 else []))
                p.on("tensor", fu, inc=(mm.inc() if c == 7 else []))
            p.on("scalar", lambda e, b=b: e.activation(sgt[:], psg[:, 0:cwg], AF.Silu),
                 wait=mm.at() + act.at() + vec.at(), inc=act.inc())
            p.on("vector", lambda e, b=b: e.tensor_tensor(ht[b][:], sgt[:], psu[:, 0:cwg], ALU.mult),
                 wait=mm.at() + act.at() + vec.at(), inc=vec.inc())
            emit_quant(p, sc, ht[b][:], hq[b][:], variant, vec, act,
                       consts, wait0=vec.at() + (sto.at(b) if k >= 2 else []))
            p.on("sync", lambda e, b=b, m=m, n=n: e.dma_start(
                out=od[m * 128:(m + 1) * 128, n * cwg:(n + 1) * cwg], in_=hq[b][:]),
                wait=vec.at(), inc=sto.inc(b))
            k += 1
    p.on("sync", lambda e: e.nop(), wait=sto.at(0) + sto.at(1))
    return p.build()


def build_combine(T=2048, sdt=BF16, name="combine"):
    p = Prog(name)
    D = 1024
    xd = p.inp("x", [T, D])
    sd2 = p.inp("s", [T, D], sdt)
    gd = p.inp("gr", [T, D], BF16)
    od = p.outp("xn2", [T, D])
    ld = DmaCtr(p, "ld", 2)
    ldsg = DmaCtr(p, "ldsg", 2)
    vec = Ctr(p, "vec")
    act = Ctr(p, "act")
    sto = DmaCtr(p, "sto", 2)
    xt = [p.sb("xt", [128, D]) for _ in range(2)]
    st = [p.sb("st", [128, D], BF16) for _ in range(2)]
    gt = [p.sb("gt", [128, D], BF16) for _ in range(2)]
    sg = [p.sb("sg", [128, D]) for _ in range(2)]
    ot = [p.sb("ot", [128, D]) for _ in range(2)]
    for i in range(T // 128):
        b = i % 2
        w = sto.at(b) if i >= 2 else []
        p.on("sync", lambda e, i=i, b=b: e.dma_start(out=xt[b][:], in_=xd[i * 128:(i + 1) * 128, :]), wait=w, inc=ld.inc(b))
        p.on("gpsimd" if sdt == F32 else "sync",
             lambda e, i=i, b=b: e.dma_start(out=st[b][:], in_=sd2[i * 128:(i + 1) * 128, :]),
             wait=(vec.at() if (sdt == F32 and i >= 2) else []),
             inc=(ldsg if sdt == F32 else ld).inc(b))
        p.on("sync", lambda e, i=i, b=b: e.dma_start(out=gt[b][:], in_=gd[i * 128:(i + 1) * 128, :]), inc=ld.inc(b))
        p.on("scalar", lambda e, b=b: e.activation(sg[b][:], gt[b][:], AF.Silu), wait=ld.at(b) + act.at() + vec.at(), inc=act.inc())
        p.on("vector", lambda e, b=b: e.tensor_mul(ot[b][:], st[b][:], sg[b][:]),
             wait=act.at() + vec.at() + (ldsg.at(b) if sdt == F32 else []), inc=vec.inc())
        p.on("vector", lambda e, b=b: e.tensor_add(ot[b][:], xt[b][:], ot[b][:]), wait=vec.at(), inc=vec.inc())
        p.on("sync", lambda e, i=i, b=b: e.dma_start(out=od[i * 128:(i + 1) * 128, :], in_=ot[b][:]),
             wait=vec.at(), inc=sto.inc(b))
    p.on("sync", lambda e: e.nop(), wait=sto.at(0) + sto.at(1))
    return p.build()


def build_l1_scan(name="l1scan"):
    p = Prog(name)
    Tf = 4096
    zd = p.inp("zT", [1024, Tf])
    hd = p.inp("htT", [1024, Tf])
    ldd = p.inp("logD", [128, 8])
    scd = p.inp("SC", [128, 128])
    bud = p.inp("BU", [128, 128])
    bdd = p.inp("BD", [128, 128])
    od = p.outp("hdT", [1024, Tf])
    ld = DmaCtr(p, "ld", 2)
    vec = Ctr(p, "vec")
    act = Ctr(p, "act")
    mm = Ctr(p, "mm")
    sto = DmaCtr(p, "sto", 2)
    ldc = Ctr(p, "ldc")
    dcs = p.sb("dc", [128, 8])
    scb = p.sb("sc", [128, 128], BF16)
    bub = p.sb("bu", [128, 128], BF16)
    bdb = p.sb("bd", [128, 128], BF16)
    ldcg = Ctr(p, "ldcg")
    p.on("sync", lambda e: e.dma_start(out=dcs[:], in_=ldd[:]), inc=ldc.inc(16))
    p.on("gpsimd", lambda e: e.dma_start(out=scb[:], in_=scd[:]), inc=ldcg.inc(16))
    p.on("gpsimd", lambda e: e.dma_start(out=bub[:], in_=bud[:]), inc=ldcg.inc(16))
    p.on("gpsimd", lambda e: e.dma_start(out=bdb[:], in_=bdd[:]), inc=ldcg.inc(16))
    p.on("scalar", lambda e: e.activation(dcs[:], dcs[:], AF.Exp), wait=ldc.at(), inc=act.inc())
    zc = [p.sb("zc", [128, Tf]) for _ in range(2)]
    hc = [p.sb("hc", [128, Tf]) for _ in range(2)]
    ac = p.sb("ac", [128, Tf])
    bc = p.sb("bc", [128, Tf])
    hf = p.sb("hf", [128, Tf])
    hsb = p.sb("h", [128, 8, Tf], BF16)
    for c in range(8):
        b = c % 2
        w = vec.at() if c >= 2 else []
        p.on("sync", lambda e, c=c, b=b: e.dma_start(out=zc[b][:], in_=zd[c * 128:(c + 1) * 128, :]), wait=w, inc=ld.inc(b))
        p.on("sync", lambda e, c=c, b=b: e.dma_start(out=hc[b][:], in_=hd[c * 128:(c + 1) * 128, :]), inc=ld.inc(b))
        p.on("vector", lambda e, b=b: e.tensor_scalar(ac[:], zc[b][:], -1.0, 1.0, ALU.mult, ALU.add),
             wait=ld.at(b) + vec.at(), inc=vec.inc())
        p.on("vector", lambda e, b=b: e.tensor_mul(bc[:], zc[b][:], hc[b][:]), wait=vec.at(), inc=vec.inc())
        p.on("vector", lambda e: e.tensor_tensor_scan(hf[:], ac[:], bc[:], 0.0, ALU.mult, ALU.add),
             wait=vec.at(), inc=vec.inc())
        p.on("vector", lambda e, c=c: e.tensor_copy(hsb[:, c, :], hf[:]), wait=vec.at(), inc=vec.inc())
    scandone = vec.v
    psd = [p.ps("psd", [128, 512]) for _ in range(2)]
    stg = [p.sb("stg", [128, 512]) for _ in range(2)]
    k = 0
    for c in range(8):
        cm1, cp1 = (c - 1) % 8, (c + 1) % 8
        for n in range(8):
            b = k % 2
            def f1(e, c=c, n=n, b=b):
                return e.matmul(psd[b][:], scb[:], hsb[:, c, n * 512:(n + 1) * 512], start=True, stop=False)
            def f2(e, cm1=cm1, n=n, b=b):
                return e.matmul(psd[b][:], bub[:], hsb[:, cm1, n * 512:(n + 1) * 512], start=False, stop=False)
            def f3(e, cp1=cp1, n=n, b=b):
                return e.matmul(psd[b][:], bdb[:], hsb[:, cp1, n * 512:(n + 1) * 512], start=False, stop=True)
            w = [(vec.s, scandone), (ldcg.s, ldcg.v)] if k == 0 else (vec.at() if k >= 2 else [])
            p.on("tensor", f1, wait=w)
            p.on("tensor", f2)
            p.on("tensor", f3, inc=mm.inc())
            def fe(e, c=c, n=n, b=b):
                return e.scalar_tensor_tensor(stg[b][:], psd[b][:], dcs[:, c:c + 1], hsb[:, c, n * 512:(n + 1) * 512],
                                              ALU.mult, ALU.add)
            p.on("vector", fe, wait=mm.at() + act.at() + (sto.at(b) if k >= 2 else []) + vec.at(), inc=vec.inc())
            p.on("sync", lambda e, c=c, n=n, b=b: e.dma_start(
                out=od[c * 128:(c + 1) * 128, n * 512:(n + 1) * 512], in_=stg[b][:]),
                wait=vec.at(), inc=sto.inc(b))
            k += 1
    p.on("sync", lambda e: e.nop(), wait=sto.at(0) + sto.at(1))
    return p.build()


def l1_shift_consts():
    SC = np.zeros((128, 128), np.float32)
    for i in range(128):
        SC[i, i] = -2.0
        if i > 0:
            SC[i, i - 1] = 1.0
        if i < 127:
            SC[i, i + 1] = 1.0
    BU = np.zeros((128, 128), np.float32)
    BU[127, 0] = 1.0
    BD = np.zeros((128, 128), np.float32)
    BD[0, 127] = 1.0
    return SC, BU, BD


def build_l1_ffn(T=2048, name="l1ffn"):
    p = Prog(name)
    xd = p.inp("xnT", [1024, T])
    u1d = p.inp("U1", [1024, 256])
    v1d = p.inp("V1", [256, 4096])
    b1d = p.inp("b1c", [128, 32])
    u2d = p.inp("U2", [4096, 256])
    v2d = p.inp("V2", [256, 1024])
    b2d = p.inp("b2", [1, 1024])
    od = p.outp("gff", [T, 1024], BF16)
    ld = Ctr(p, "ld")
    mm = Ctr(p, "mm")
    vec = Ctr(p, "vec")
    act = Ctr(p, "act")
    sto = DmaCtr(p, "sto", 2)
    xsb = p.sb("x", [128, 8, T], BF16)
    u1b = p.sb("u1", [128, 8, 256], BF16)
    v1b = p.sb("v1", [128, 2, 4096], BF16)
    u2b = p.sb("u2", [128, 32, 256], BF16)
    v2b = p.sb("v2", [128, 2, 1024], BF16)
    b1b = p.sb("b1", [128, 32])
    b2b = p.sb("b2", [1, 1024], BF16)
    onesb = p.sb("ones", [1, 128], BF16)
    ldg = Ctr(p, "ldg")
    ini = Ctr(p, "ini")
    for c in range(8):
        p.on("gpsimd", lambda e, c=c: e.dma_start(out=xsb[:, c, :], in_=xd[c * 128:(c + 1) * 128, :]), inc=ldg.inc(16))
        p.on("gpsimd", lambda e, c=c: e.dma_start(out=u1b[:, c, :], in_=u1d[c * 128:(c + 1) * 128, :]), inc=ldg.inc(16))
    for c in range(2):
        p.on("gpsimd", lambda e, c=c: e.dma_start(out=v1b[:, c, :], in_=v1d[c * 128:(c + 1) * 128, :]), inc=ldg.inc(16))
        p.on("gpsimd", lambda e, c=c: e.dma_start(out=v2b[:, c, :], in_=v2d[c * 128:(c + 1) * 128, :]), inc=ldg.inc(16))
    for c in range(32):
        p.on("gpsimd", lambda e, c=c: e.dma_start(out=u2b[:, c, :], in_=u2d[c * 128:(c + 1) * 128, :]), inc=ldg.inc(16))
    p.on("sync", lambda e: e.dma_start(out=b1b[:], in_=b1d[:]), inc=ld.inc(16))
    p.on("gpsimd", lambda e: e.dma_start(out=b2b[:], in_=b2d[:]), inc=ldg.inc(16))
    p.on("vector", lambda e: e.memset(onesb[:], 1.0), inc=ini.inc(1))
    ldall_w = [(ld.s, ld.v), (ldg.s, ldg.v), (ini.s, ini.v)]
    psp = p.ps("psp", [128, 512])
    psh = [p.ps("psh", [128, 512]) for _ in range(2)]
    psq = p.ps("psq", [128, 512])
    psg = [p.ps("psg", [128, 512]) for _ in range(2)]
    ptb = p.sb("pt", [128, 2, 512], BF16)
    htb = p.sb("ht", [128, 32, 512], BF16)
    qtb = p.sb("qt", [128, 2, 512], BF16)
    hb0 = p.sb("hb0", [128, 512])
    x2 = p.sb("x2", [128, 512])
    t1 = p.sb("t1", [128, 512])
    t2 = p.sb("t2", [128, 512])
    sgm = p.sb("sgm", [128, 512])
    stg = [p.sb("stg", [128, 512], BF16) for _ in range(2)]
    for u in range(T // 512):
        for rc in range(2):
            for c in range(8):
                def f(e, rc=rc, c=c, u=u):
                    return e.matmul(psp[:], u1b[:, c, rc * 128:(rc + 1) * 128], xsb[:, c, u * 512:(u + 1) * 512],
                                    start=(c == 0), stop=(c == 7))
                w = []
                if c == 0:
                    w = ldall_w if u == 0 and rc == 0 else act.at()
                p.on("tensor", f, wait=w, inc=(mm.inc() if c == 7 else []))
            p.on("scalar", lambda e, rc=rc: e.activation(ptb[:, rc, :], psp[:], AF.Copy),
                 wait=mm.at() + act.at(), inc=act.inc())
        for fc in range(32):
            bh = fc % 2
            for rc in range(2):
                def f(e, fc=fc, rc=rc, bh=bh):
                    return e.matmul(psh[bh][:], v1b[:, rc, fc * 128:(fc + 1) * 128], ptb[:, rc, :],
                                    start=(rc == 0), stop=(rc == 1))
                p.on("tensor", f, wait=(act.at() if rc == 0 else []) + (vec.at() if rc == 0 and fc >= 2 else []),
                     inc=(mm.inc() if rc == 1 else []))
            p.on("vector", lambda e, fc=fc, bh=bh: e.tensor_scalar(hb0[:], psh[bh][:], b1b[:, fc:fc + 1], None, ALU.add),
                 wait=mm.at() + vec.at() + act.at(), inc=vec.inc())
            p.on("scalar", lambda e: e.activation(x2[:], hb0[:], AF.Square), wait=vec.at() + act.at(), inc=act.inc())
            p.on("vector", lambda e: e.scalar_tensor_tensor(t1[:], x2[:], 0.07135481627961313, hb0[:], ALU.mult, ALU.mult),
                 wait=act.at() + vec.at(), inc=vec.inc())
            p.on("vector", lambda e: e.scalar_tensor_tensor(t2[:], hb0[:], 1.5957691216057308, t1[:], ALU.mult, ALU.add),
                 wait=vec.at(), inc=vec.inc())
            p.on("scalar", lambda e: e.activation(sgm[:], t2[:], AF.Sigmoid), wait=vec.at() + act.at(), inc=act.inc())
            p.on("vector", lambda e, fc=fc: e.tensor_mul(htb[:, fc, :], hb0[:], sgm[:]),
                 wait=act.at() + vec.at(), inc=vec.inc())
        for rc in range(2):
            for fc in range(32):
                def f(e, rc=rc, fc=fc):
                    return e.matmul(psq[:], u2b[:, fc, rc * 128:(rc + 1) * 128], htb[:, fc, :],
                                    start=(fc == 0), stop=(fc == 31))
                p.on("tensor", f, wait=(vec.at() + act.at() if fc == 0 else []), inc=(mm.inc() if fc == 31 else []))
            p.on("scalar", lambda e, rc=rc: e.activation(qtb[:, rc, :], psq[:], AF.Copy),
                 wait=mm.at() + act.at(), inc=act.inc())
        for ts in range(4):
            bg = ts % 2
            for dc in range(2):
                for rc in range(2):
                    def f(e, ts=ts, dc=dc, rc=rc, bg=bg):
                        return e.matmul(psg[bg][:], qtb[:, rc, ts * 128:(ts + 1) * 128], v2b[:, rc, dc * 512:(dc + 1) * 512],
                                        start=(rc == 0), stop=False)
                    p.on("tensor", f, wait=(act.at() + (vec.at() if u + ts + dc else []) if rc == 0 else []))
                def fb(e, dc=dc, bg=bg):
                    return e.matmul(psg[bg][:], onesb[:], b2b[:, dc * 512:(dc + 1) * 512], start=False, stop=True)
                p.on("tensor", fb, inc=mm.inc())
                kst = u * 8 + ts * 2 + dc
                def fe(e, bg=bg, kst=kst):
                    return e.tensor_copy(stg[kst % 2][:], psg[bg][:])
                p.on("vector", fe, wait=mm.at() + (sto.at(kst) if kst >= 2 else []) + vec.at(), inc=vec.inc())
                p.on("sync", lambda e, u=u, ts=ts, dc=dc, kst=kst: e.dma_start(
                    out=od[u * 512 + ts * 128:u * 512 + (ts + 1) * 128, dc * 512:(dc + 1) * 512], in_=stg[kst % 2][:]),
                    wait=vec.at(), inc=sto.inc(kst))
    p.on("sync", lambda e: e.nop(), wait=sto.at(0) + sto.at(1))
    return p.build()


def build_l3_prep(name="l3prep"):
    p = Prog(name)
    S = 4096
    qd = p.inp("q", [S, 512])
    kd = p.inp("k", [S, 512])
    rad = p.inp("rawa", [S, 8])
    ald = p.inp("A_log", [128, 8])
    qod = p.outp("qn", [S, 512], BF16)
    kod = p.outp("kn", [S, 512], BF16)
    dod = p.outp("decay", [S, 8])
    ld = DmaCtr(p, "ld", 2)
    vec = Ctr(p, "vec")
    act = Ctr(p, "act")
    sto = DmaCtr(p, "sto", 2)
    ldc = Ctr(p, "ldc")
    ea = p.sb("ea", [128, 8])
    p.on("sync", lambda e: e.dma_start(out=ea[:], in_=ald[:]), inc=ldc.inc(16))
    p.on("scalar", lambda e: e.activation(ea[:], ea[:], AF.Exp), wait=ldc.at(), inc=act.inc())
    qt = [p.sb("qt", [128, 512]) for _ in range(2)]
    kt = [p.sb("kt", [128, 512]) for _ in range(2)]
    rat = [p.sb("rat", [128, 8]) for _ in range(2)]
    x2 = p.sb("x2", [128, 512])
    ss = p.sb("ss", [128, 8])
    rn = p.sb("rn", [128, 8])
    rnq = p.sb("rnq", [128, 8])
    qo = [p.sb("qo", [128, 512], BF16) for _ in range(2)]
    ko = [p.sb("ko", [128, 512], BF16) for _ in range(2)]
    do_ = [p.sb("do", [128, 8]) for _ in range(2)]
    az = p.sb("az", [128, 8])
    ez = p.sb("ez", [128, 8])
    lz = p.sb("lz", [128, 8])
    mx = p.sb("mx", [128, 8])
    sp = p.sb("sp", [128, 8])
    for i in range(S // 128):
        b = i % 2
        w = sto.at(b) if i >= 2 else []
        p.on("sync", lambda e, i=i, b=b: e.dma_start(out=qt[b][:], in_=qd[i * 128:(i + 1) * 128, :]), wait=w, inc=ld.inc(b))
        p.on("sync", lambda e, i=i, b=b: e.dma_start(out=kt[b][:], in_=kd[i * 128:(i + 1) * 128, :]), inc=ld.inc(b))
        p.on("sync", lambda e, i=i, b=b: e.dma_start(out=rat[b][:], in_=rad[i * 128:(i + 1) * 128, :]), inc=ld.inc(b))
        for (src, dst, isq) in ((qt, qo, True), (kt, ko, False)):
            p.on("scalar", lambda e, b=b, src=src: e.activation(x2[:], src[b][:], AF.Square),
                 wait=ld.at(b) + act.at() + vec.at(), inc=act.inc())
            p.on("vector", lambda e: e.reduce_sum(ss[:], x2[:].rearrange("p (h k) -> p h k", k=64), axis=AX.X),
                 wait=act.at() + vec.at(), inc=vec.inc())
            p.on("vector", lambda e: e.tensor_scalar(ss[:], ss[:], 1e-6, None, ALU.add), wait=vec.at(), inc=vec.inc())
            p.on("scalar", lambda e: e.activation(ss[:], ss[:], AF.Sqrt), wait=vec.at() + act.at(), inc=act.inc())
            p.on("vector", lambda e: e.reciprocal(rn[:], ss[:]), wait=act.at() + vec.at(), inc=vec.inc())
            if isq:
                p.on("vector", lambda e: e.tensor_scalar(rnq[:], rn[:], 0.125, None, ALU.mult), wait=vec.at(), inc=vec.inc())
                rr = rnq
            else:
                rr = rn
            def fn(e, b=b, src=src, dst=dst, rr=rr):
                return e.tensor_tensor(dst[b][:].rearrange("p (h k) -> p h k", k=64),
                                       src[b][:].rearrange("p (h k) -> p h k", k=64),
                                       rr[:].unsqueeze(2).broadcast_to([128, 8, 64]), ALU.mult)
            p.on("vector", fn, wait=vec.at(), inc=vec.inc())
        p.on("scalar", lambda e, b=b: e.activation(az[:], rat[b][:], AF.Abs), wait=ld.at(b) + act.at() + vec.at(), inc=act.inc())
        p.on("vector", lambda e: e.tensor_scalar(az[:], az[:], -1.0, None, ALU.mult), wait=act.at() + vec.at(), inc=vec.inc())
        p.on("scalar", lambda e: e.activation(ez[:], az[:], AF.Exp), wait=vec.at() + act.at(), inc=act.inc())
        p.on("vector", lambda e: e.tensor_scalar(ez[:], ez[:], 1.0, None, ALU.add), wait=act.at() + vec.at(), inc=vec.inc())
        p.on("scalar", lambda e: e.activation(lz[:], ez[:], AF.Ln), wait=vec.at() + act.at(), inc=act.inc())
        p.on("vector", lambda e, b=b: e.tensor_scalar(mx[:], rat[b][:], 0.0, None, ALU.max), wait=act.at() + vec.at(), inc=vec.inc())
        p.on("vector", lambda e: e.tensor_add(sp[:], mx[:], lz[:]), wait=vec.at(), inc=vec.inc())
        p.on("vector", lambda e, b=b: e.scalar_tensor_tensor(do_[b][:], sp[:], -1.0, ea[:], ALU.mult, ALU.mult),
             wait=vec.at() + [(ldc.s, ldc.v)][0:0] + act.at(), inc=vec.inc())
        p.on("sync", lambda e, i=i, b=b: e.dma_start(out=qod[i * 128:(i + 1) * 128, :], in_=qo[b][:]), wait=vec.at(), inc=sto.inc(b))
        p.on("sync", lambda e, i=i, b=b: e.dma_start(out=kod[i * 128:(i + 1) * 128, :], in_=ko[b][:]), inc=sto.inc(b))
        p.on("sync", lambda e, i=i, b=b: e.dma_start(out=dod[i * 128:(i + 1) * 128, :], in_=do_[b][:]), inc=sto.inc(b))
    p.on("sync", lambda e: e.nop(), wait=sto.at(0) + sto.at(1))
    return p.build()


def build_l3_gate(T=2048, name="l3gate"):
    p = Prog(name)
    xd = p.inp("xnT", [1024, T])
    gwd = p.inp("gwT", [1024, 8])
    lbd = p.inp("lb", [1, 8])
    swd = p.outp("selw", [T, 8])
    ptd = p.outp("partials", [2, 8])
    ld = Ctr(p, "ld")
    mm = Ctr(p, "mm")
    vec = Ctr(p, "vec")
    act = Ctr(p, "act")
    sto = Ctr(p, "sto")
    xsb = p.sb("x", [128, 8, T], BF16)
    gwb = p.sb("gw", [128, 8, 8], BF16)
    lbb = p.sb("lb", [1, 8], BF16)
    onesb = p.sb("ones", [1, 128], BF16)
    onescol = p.sb("onescol", [128, 1])
    ini = Ctr(p, "ini")
    for c in range(8):
        p.on("gpsimd", lambda e, c=c: e.dma_start(out=xsb[:, c, :], in_=xd[c * 128:(c + 1) * 128, :]), inc=ld.inc(16))
        p.on("gpsimd", lambda e, c=c: e.dma_start(out=gwb[:, c, :], in_=gwd[c * 128:(c + 1) * 128, :]), inc=ld.inc(16))
    p.on("gpsimd", lambda e: e.dma_start(out=lbb[:], in_=lbd[:]), inc=ld.inc(16))
    p.on("vector", lambda e: e.memset(onesb[:], 1.0), inc=ini.inc(1))
    p.on("vector", lambda e: e.memset(onescol[:], 1.0), inc=ini.inc(1))
    ldall_w = [(ld.s, ld.v), (ini.s, ini.v)]
    pss = p.ps("pss", [128, 8])
    scor = p.sb("scor", [128, 16, 8])
    for tc in range(T // 128):
        for c in range(8):
            def f(e, tc=tc, c=c):
                return e.matmul(pss[:], xsb[:, c, tc * 128:(tc + 1) * 128], gwb[:, c, :],
                                start=(c == 0), stop=False)
            p.on("tensor", f, wait=(ldall_w if tc == 0 and c == 0 else (act.at() if c == 0 else [])))
        p.on("tensor", lambda e: e.matmul(pss[:], onesb[:], lbb[:], start=False, stop=True), inc=mm.inc())
        p.on("scalar", lambda e, tc=tc: e.activation(scor[:, tc, :], pss[:], AF.Sigmoid), wait=mm.at(), inc=act.inc())
    m1 = p.sb("m1", [128, 16])
    m2 = p.sb("m2", [128, 16])
    eqm = p.sb("eqm", [128, 16, 8])
    s2 = p.sb("s2", [128, 16, 8])
    sel = p.sb("sel", [128, 16, 8])
    rd2 = p.sb("rd", [128, 16])
    wv = p.sb("wv", [128, 16, 8])
    swb = p.sb("swb", [128, 16, 8])
    p.on("vector", lambda e: e.reduce_max(m1[:], scor[:], axis=AX.X), wait=act.at() + vec.at(), inc=vec.inc())
    p.on("vector", lambda e: e.tensor_tensor(eqm[:], scor[:], m1[:].unsqueeze(2).broadcast_to([128, 16, 8]), ALU.is_equal),
         wait=vec.at(), inc=vec.inc())
    p.on("vector", lambda e: e.scalar_tensor_tensor(s2[:], eqm[:], -1e30, scor[:], ALU.mult, ALU.add),
         wait=vec.at(), inc=vec.inc())
    p.on("vector", lambda e: e.reduce_max(m2[:], s2[:], axis=AX.X), wait=vec.at(), inc=vec.inc())
    p.on("vector", lambda e: e.tensor_tensor(sel[:], scor[:], m2[:].unsqueeze(2).broadcast_to([128, 16, 8]), ALU.is_ge),
         wait=vec.at(), inc=vec.inc())
    p.on("vector", lambda e: e.tensor_add(m1[:], m1[:], m2[:]), wait=vec.at(), inc=vec.inc())
    p.on("vector", lambda e: e.tensor_scalar(m1[:], m1[:], 1e-8, None, ALU.add), wait=vec.at(), inc=vec.inc())
    p.on("vector", lambda e: e.reciprocal(rd2[:], m1[:]), wait=vec.at(), inc=vec.inc())
    p.on("vector", lambda e: e.tensor_tensor(wv[:], scor[:], rd2[:].unsqueeze(2).broadcast_to([128, 16, 8]), ALU.mult),
         wait=vec.at(), inc=vec.inc())
    p.on("vector", lambda e: e.tensor_mul(swb[:], sel[:], wv[:]), wait=vec.at(), inc=vec.inc())
    psp1 = p.ps("psp1", [1, 128])
    psp2 = p.ps("psp2", [1, 128])
    p.on("tensor", lambda e: e.matmul(psp1[:], onescol[:], swb[:].rearrange("p a b -> p (a b)"), start=True, stop=True),
         wait=vec.at(), inc=mm.inc())
    p.on("tensor", lambda e: e.matmul(psp2[:], onescol[:], sel[:].rearrange("p a b -> p (a b)"), start=True, stop=True),
         inc=mm.inc())
    pt1 = p.sb("pt1", [1, 128])
    pt2 = p.sb("pt2", [1, 128])
    ptw = p.sb("ptw", [1, 8])
    ptc = p.sb("ptc", [1, 8])
    p.on("vector", lambda e: e.tensor_copy(pt1[:], psp1[:]), wait=mm.at() + vec.at(), inc=vec.inc())
    p.on("vector", lambda e: e.tensor_copy(pt2[:], psp2[:]), wait=vec.at(), inc=vec.inc())
    p.on("vector", lambda e: e.reduce_sum(ptw[:], pt1[:].rearrange("p (a b) -> p b a", b=8), axis=AX.X),
         wait=vec.at(), inc=vec.inc())
    p.on("vector", lambda e: e.reduce_sum(ptc[:], pt2[:].rearrange("p (a b) -> p b a", b=8), axis=AX.X),
         wait=vec.at(), inc=vec.inc())
    p.on("sync", lambda e: e.dma_start(out=swd.rearrange("(a p) b -> p a b", p=128), in_=swb[:]),
         wait=vec.at(), inc=sto.inc(16))
    p.on("sync", lambda e: e.dma_start(out=ptd[0:1, :], in_=ptw[:]), inc=sto.inc(16))
    p.on("sync", lambda e: e.dma_start(out=ptd[1:2, :], in_=ptc[:]), inc=sto.inc(16))
    p.on("sync", lambda e: e.nop(), wait=sto.at())
    return p.build()


def build_moe(T=2048, name="moe"):
    p = Prog(name)
    xd = p.inp("xnT", [1024, T])
    swd = p.inp("selw", [T, 8])
    ptd = p.inp("partials", [8, 16])
    egd = p.inp("egwT", [1024, 5120])
    eud = p.inp("euwT", [1024, 5120])
    edd = p.inp("edwR", [5120, 1024])
    od = p.outp("gmoe", [T, 1024], BF16)
    ld = Ctr(p, "ld")
    wld = DmaCtr(p, "wld", 2)
    mm = Ctr(p, "mm")
    vec = Ctr(p, "vec")
    act = Ctr(p, "act")
    sto = DmaCtr(p, "sto", 2)
    xsb = p.sb("x", [128, 8, T], BF16)
    swb = p.sb("sw", [128, 16, 8])
    onec = p.sb("onec", [128, 1])
    ptb = p.sb("pt", [1, 128])
    onesr = p.sb("onesr", [1, 128])
    lds = Ctr(p, "lds")
    ini = Ctr(p, "ini")
    for c in range(8):
        p.on("gpsimd", lambda e, c=c: e.dma_start(out=xsb[:, c, :], in_=xd[c * 128:(c + 1) * 128, :]), inc=ld.inc(16))
    p.on("sync", lambda e: e.dma_start(out=swb[:], in_=swd.rearrange("(a p) b -> p a b", p=128)), inc=lds.inc(16))
    p.on("sync", lambda e: e.dma_start(out=ptb[:], in_=ptd.rearrange("a b -> (a b)").unsqueeze(0)), inc=lds.inc(16))
    p.on("vector", lambda e: e.memset(onesr[:], 1.0), inc=ini.inc(1))
    p.on("vector", lambda e: e.memset(onec[:], 1.0), inc=ini.inc(1))
    sums = p.sb("sums", [1, 16])
    cm = p.sb("cm", [1, 8])
    rc_ = p.sb("rcp", [1, 8])
    wmean = p.sb("wmean", [1, 8])
    wmb = p.sb("wmb", [128, 8])
    pswm = p.ps("pswm", [128, 8])
    p.on("vector", lambda e: e.reduce_sum(sums[:], ptb[:].rearrange("p (c k) -> p k c", k=16), axis=AX.X),
         wait=lds.at() + ini.at() + vec.at(), inc=vec.inc())
    p.on("vector", lambda e: e.tensor_scalar(cm[:], sums[0:1, 8:16], 1.0, None, ALU.max), wait=vec.at(), inc=vec.inc())
    p.on("vector", lambda e: e.reciprocal(rc_[:], cm[:]), wait=vec.at(), inc=vec.inc())
    p.on("vector", lambda e: e.tensor_mul(wmean[:], sums[0:1, 0:8], rc_[:]), wait=vec.at(), inc=vec.inc())
    wmbf = p.sb("wmbf", [1, 8], BF16)
    onesbf = p.sb("onesbf", [1, 128], BF16)
    p.on("vector", lambda e: e.memset(onesbf[:], 1.0), inc=ini.inc(1))
    p.on("vector", lambda e: e.tensor_copy(wmbf[:], wmean[:]), wait=vec.at(), inc=vec.inc())
    p.on("tensor", lambda e: e.matmul(pswm[:], onesbf[:], wmbf[:], start=True, stop=True),
         wait=vec.at() + ini.at(), inc=mm.inc())
    p.on("vector", lambda e: e.tensor_copy(wmb[:], pswm[:]), wait=mm.at() + vec.at(), inc=vec.inc())
    p.on("vector", lambda e: e.tensor_tensor(swb[:], swb[:], wmb[:].unsqueeze(1).broadcast_to([128, 16, 8]), ALU.mult),
         wait=vec.at(), inc=vec.inc())
    ldall = ld.v
    psg = p.ps("psg", [128, 512])
    psu = p.ps("psu", [128, 512])
    psy = p.ps("psy", [128, 512])
    sgt = p.sb("sgt", [128, 512])
    hhsb = p.sb("hh", [128, 4, 512], BF16)
    gacc = p.sb("gacc", [128, 16, 1024])
    stg = [p.sb("stg", [128, 512], BF16) for _ in range(2)]
    wgb = [p.sb("wgb", [128, 8, 512], BF16) for _ in range(2)]
    wub = [p.sb("wub", [128, 8, 512], BF16) for _ in range(2)]
    wdb = [p.sb("wdb", [128, 4, 1024], BF16) for _ in range(2)]
    mm_end = {}
    for e_ in range(10):
        b = e_ % 2
        wwait = ([(mm.s, mm_end[e_ - 2])] if e_ >= 2 else [])
        for c in range(8):
            p.on("gpsimd", lambda e, c=c, b=b, e_=e_: e.dma_start(
                out=wgb[b][:, c, :], in_=egd[c * 128:(c + 1) * 128, e_ * 512:(e_ + 1) * 512]),
                wait=(wwait if c == 0 else []), inc=wld.inc(b))
            p.on("gpsimd", lambda e, c=c, b=b, e_=e_: e.dma_start(
                out=wub[b][:, c, :], in_=eud[c * 128:(c + 1) * 128, e_ * 512:(e_ + 1) * 512]), inc=wld.inc(b))
        for ic in range(4):
            p.on("gpsimd", lambda e, ic=ic, b=b, e_=e_: e.dma_start(
                out=wdb[b][:, ic, :], in_=edd[e_ * 512 + ic * 128:e_ * 512 + (ic + 1) * 128, :]), inc=wld.inc(b))
        for u in range(4):
            tcbase = u * 4
            for ic in range(4):
                for c in range(8):
                    def fg(e, c=c, ic=ic, u=u, b=b):
                        return e.matmul(psg[:], wgb[b][:, c, ic * 128:(ic + 1) * 128],
                                        xsb[:, c, u * 512:(u + 1) * 512], start=(c == 0), stop=(c == 7))
                    def fu(e, c=c, ic=ic, u=u, b=b):
                        return e.matmul(psu[:], wub[b][:, c, ic * 128:(ic + 1) * 128],
                                        xsb[:, c, u * 512:(u + 1) * 512], start=(c == 0), stop=(c == 7))
                    w = []
                    if c == 0:
                        w = vec.at() + ([(ld.s, ldall)] if e_ == 0 and u == 0 and ic == 0 else []) + wld.at(b)
                    p.on("tensor", fg, wait=w, inc=(mm.inc() if c == 7 else []))
                    p.on("tensor", fu, inc=(mm.inc() if c == 7 else []))
                p.on("scalar", lambda e: e.activation(sgt[:], psg[:], AF.Silu),
                     wait=mm.at() + act.at() + vec.at(), inc=act.inc())
                p.on("vector", lambda e, ic=ic: e.tensor_tensor(hhsb[:, ic, :], sgt[:], psu[:], ALU.mult),
                     wait=mm.at() + act.at() + vec.at(), inc=vec.inc())
            for ts in range(4):
                for doc in range(2):
                    for ic in range(4):
                        def fy(e, ic=ic, ts=ts, doc=doc, b=b):
                            return e.matmul(psy[:], hhsb[:, ic, ts * 128:(ts + 1) * 128],
                                            wdb[b][:, ic, doc * 512:(doc + 1) * 512],
                                            start=(ic == 0), stop=(ic == 3))
                        p.on("tensor", fy, wait=(vec.at() if ic == 0 else []), inc=(mm.inc() if ic == 3 else []))
                    tc = tcbase + ts
                    if e_ < 8:
                        scal = swb[:, tc:tc + 1, e_:e_ + 1].rearrange("p a b -> p (a b)")
                    else:
                        scal = onec[:]
                    if e_ == 0:
                        def fe(e, tc=tc, doc=doc, scal=scal):
                            return e.tensor_scalar(gacc[:, tc, doc * 512:(doc + 1) * 512], psy[:], scal, None, ALU.mult)
                    else:
                        def fe(e, tc=tc, doc=doc, scal=scal):
                            return e.scalar_tensor_tensor(gacc[:, tc, doc * 512:(doc + 1) * 512], psy[:], scal,
                                                          gacc[:, tc, doc * 512:(doc + 1) * 512], ALU.mult, ALU.add)
                    p.on("vector", fe, wait=mm.at() + vec.at(), inc=vec.inc())
        mm_end[e_] = mm.v
    k = 0
    for tc in range(16):
        for doc in range(2):
            bk = k % 2
            def fc(e, tc=tc, doc=doc, bk=bk):
                return e.tensor_copy(stg[bk][:], gacc[:, tc, doc * 512:(doc + 1) * 512])
            p.on("vector", fc, wait=vec.at() + (sto.at(bk) if k >= 2 else []), inc=vec.inc())
            p.on("sync", lambda e, tc=tc, doc=doc, bk=bk: e.dma_start(
                out=od[tc * 128:(tc + 1) * 128, doc * 512:(doc + 1) * 512], in_=stg[bk][:]),
                wait=vec.at(), inc=sto.inc(bk))
            k += 1
    p.on("sync", lambda e: e.nop(), wait=sto.at(0) + sto.at(1))
    return p.build()


def build_deltanet(name="dnet"):
    p = Prog(name, scratch=4096)
    Tf, C, NCH = 4096, 64, 64
    qTd = p.inp("qT", [512, Tf], BF16)
    kTd = p.inp("kT", [512, Tf], BF16)
    vcwd = p.inp("v_cw", [4, 128, NCH, 64], BF16)
    kcwd = p.inp("k_cw", [4, 128, NCH, 64], BF16)
    bcwd = p.inp("beta_cw", [4, 128, NCH])
    dcwd = p.inp("decay_cw", [4, 128, NCH])
    dTd = p.inp("decayT", [8, Tf])
    bTd = p.inp("betaT", [8, Tf])
    trild = p.inp("trilI2", [128, 64])
    mLTd = p.inp("mLT", [128, 64], BF16)
    mPTd = p.inp("mPT", [128, 64], BF16)
    mPd = p.inp("mP", [128, 64], BF16)
    od = p.outp("o", [Tf, 512], BF16)
    gld = p.dram("gl_scr", [8, Tf])
    glastd = p.dram("glast_scr", [8, NCH])

    ld = Ctr(p, "ld")
    ldg = Ctr(p, "ldg")
    mm = Ctr(p, "mm")
    vec = Ctr(p, "vec")
    act = Ctr(p, "act")
    sto = DmaCtr(p, "sto", 2)

    dsb_t = p.sb("dsb", [8, Tf])
    ggsb_t = p.sb("ggsb", [8, Tf])
    glsb_t = p.sb("glsb", [8, Tf])
    ones8_t = p.sb("ones8", [8, Tf])
    dsb = dsb_t[:]
    ggsb = ggsb_t[:]
    glsb = glsb_t[:]
    ones8 = ones8_t[:]
    glast8 = p.sb("glast8", [8, NCH])
    cbt = p.sb("cbt", [8, NCH])
    trilb = p.sb("tril", [128, 64])
    mLT = p.sb("mLT", [128, 64], BF16)
    mPT = p.sb("mPT", [128, 64], BF16)
    mP = p.sb("mP", [128, 64], BF16)
    p.on("sync", lambda e: e.dma_start(out=dsb, in_=dTd[:]), inc=ld.inc(16))
    p.on("sync", lambda e: e.dma_start(out=trilb[:], in_=trild[:]), inc=ld.inc(16))
    p.on("sync", lambda e: e.dma_start(out=mLT[:], in_=mLTd[:]), inc=ld.inc(16))
    p.on("sync", lambda e: e.dma_start(out=mPT[:], in_=mPTd[:]), inc=ld.inc(16))
    p.on("sync", lambda e: e.dma_start(out=mP[:], in_=mPd[:]), inc=ld.inc(16))
    p.on("vector", lambda e: e.memset(ones8, 1.0), inc=vec.inc())
    p.on("vector", lambda e: e.tensor_tensor_scan(ggsb, ones8, dsb, 0.0, ALU.mult, ALU.add),
         wait=ld.at() + vec.at(), inc=vec.inc())
    p.on("vector", lambda e: e.memset(cbt[:, 0:1], 0.0), wait=vec.at(), inc=vec.inc())
    p.on("vector", lambda e: e.tensor_copy(cbt[:, 1:NCH], ggsb[:, C - 1:Tf - C:C]), wait=vec.at(), inc=vec.inc())
    p.on("vector", lambda e: e.tensor_tensor(glsb.rearrange("p (a b) -> p a b", b=C),
                                             ggsb.rearrange("p (a b) -> p a b", b=C),
                                             cbt[:].unsqueeze(2).broadcast_to([8, NCH, C]), ALU.subtract),
         wait=vec.at(), inc=vec.inc())
    p.on("vector", lambda e: e.tensor_copy(glast8[:], glsb[:, C - 1:Tf:C]), wait=vec.at(), inc=vec.inc())
    p.on("sync", lambda e: e.dma_start(out=gld[:], in_=glsb), wait=vec.at(), inc=ld.inc(16))
    p.on("sync", lambda e: e.dma_start(out=glastd[:], in_=glast8[:]), inc=ld.inc(16))
    gl_dram = ld.v

    qTp = p.sb("qTp", [128, Tf], BF16)
    kTp = p.sb("kTp", [128, Tf], BF16)
    vp = p.sb("vp", [128, NCH, 64], BF16)
    kp = p.sb("kp", [128, NCH, 64], BF16)
    bcw = p.sb("bcw", [128, NCH])
    dcw = p.sb("dcw", [128, NCH])
    gb = p.sb("gb", [128, Tf])
    bb = p.sb("bb", [128, Tf], BF16)
    glastb = p.sb("glastb", [128, NCH])
    egb = p.sb("egb", [128, Tf], BF16)
    ltr = p.sb("ltr", [128, Tf], BF16)
    ltrn = p.sb("ltrn", [128, Tf], BF16)
    dtmp = p.sb("dtmp", [128, Tf // 2])
    LTm = p.sb("LTmv", [128, Tf], BF16)
    PTm = p.sb("PTmv", [128, Tf], BF16)
    Pmv = p.sb("Pmv", [128, Tf], BF16)
    kbegT = p.sb("kbegT", [128, Tf], BF16)
    qpT = p.sb("qpT", [128, Tf], BF16)
    gpart = p.sb("gpart", [128, NCH])
    ftc = p.sb("ftc", [128, NCH])
    egl = p.sb("egl", [128, NCH])
    vb = p.sb("vbv", [128, NCH, 64], BF16)
    ksc = p.sb("kscv", [128, NCH, 64], BF16)
    Sf = p.sb("Sf", [128, 64])
    Sbf = p.sb("Sbf", [128, 64], BF16)
    Usb = [p.sb("Usb", [128, 64], BF16) for _ in range(2)]
    Pk = p.sb("Pk", [128, 64], BF16)
    PkT = p.sb("PkT", [128, 64], BF16)
    ATs = p.sb("ATs", [128, 64], BF16)
    op_ = p.sb("op", [128, NCH, 64], BF16)

    ps = [p.ps(f"ps{i}", [128, 512]) for i in range(8)]
    psS, psAT, psW, psU, psQ1, psQ2, psO, psSu = ps

    for hp in range(4):
        hA, hB = 2 * hp, 2 * hp + 1
        wpl = (vec.at() + mm.at() + act.at()) if hp > 0 else []
        p.on("sync", lambda e, hp=hp: e.dma_start(out=qTp[:], in_=qTd[hp * 128:(hp + 1) * 128, :]), wait=wpl, inc=ld.inc(16))
        p.on("sync", lambda e, hp=hp: e.dma_start(out=kTp[:], in_=kTd[hp * 128:(hp + 1) * 128, :]), inc=ld.inc(16))
        p.on("sync", lambda e, hp=hp: e.dma_start(out=vp[:], in_=vcwd[hp]), inc=ld.inc(16))
        p.on("sync", lambda e, hp=hp: e.dma_start(out=kp[:], in_=kcwd[hp]), inc=ld.inc(16))
        p.on("sync", lambda e, hp=hp: e.dma_start(out=bcw[:], in_=bcwd[hp]), inc=ld.inc(16))
        p.on("sync", lambda e, hp=hp: e.dma_start(out=dcw[:], in_=dcwd[hp]), inc=ld.inc(16))
        p.on("sync", lambda e, hA=hA: e.dma_start(out=gb[0:64, :], in_=gld[hA].unsqueeze(0).broadcast_to([64, Tf])),
             wait=[(ld.s, gl_dram)], inc=ld.inc(16))
        p.on("sync", lambda e, hB=hB: e.dma_start(out=gb[64:128, :], in_=gld[hB].unsqueeze(0).broadcast_to([64, Tf])), inc=ld.inc(16))
        p.on("gpsimd", lambda e, hA=hA: e.dma_start(out=bb[0:64, :], in_=bTd[hA].unsqueeze(0).broadcast_to([64, Tf])),
             wait=(vec.at() + mm.at() + act.at()) if hp > 0 else [], inc=ldg.inc(16))
        p.on("gpsimd", lambda e, hB=hB: e.dma_start(out=bb[64:128, :], in_=bTd[hB].unsqueeze(0).broadcast_to([64, Tf])), inc=ldg.inc(16))
        p.on("sync", lambda e, hA=hA: e.dma_start(out=glastb[0:64, :], in_=glastd[hA].unsqueeze(0).broadcast_to([64, NCH])), inc=ld.inc(16))
        p.on("sync", lambda e, hB=hB: e.dma_start(out=glastb[64:128, :], in_=glastd[hB].unsqueeze(0).broadcast_to([64, NCH])), inc=ld.inc(16))
        ldpair = ld.v
        ldgpair = ldg.v
        for half in (0, 1):
            o1, o2 = half * 64, half * 64 + 64
            def fgp(e, o1=o1, o2=o2):
                return e.matmul(psU[o1:o2, 0:NCH], trilb[o1:o2, 0:64], dcw[o1:o2, :], start=True, stop=True)
            p.on("tensor", fgp, wait=(([(ld.s, ldpair)] + act.at() + vec.at()) if half == 0 else []),
                 inc=(mm.inc() if half == 1 else []))
        p.on("scalar", lambda e: e.activation(gpart[:], psU[:, 0:NCH], AF.Copy), wait=mm.at() + act.at(), inc=act.inc())
        p.on("scalar", lambda e: e.activation(egl[:], glastb[:], AF.Exp), wait=act.at(), inc=act.inc())
        p.on("vector", lambda e: e.tensor_sub(ftc[:], glastb[:], gpart[:]), wait=act.at() + vec.at(), inc=vec.inc())
        p.on("scalar", lambda e: e.activation(ftc[:], ftc[:], AF.Exp), wait=vec.at() + act.at(), inc=act.inc())
        p.on("scalar", lambda e: e.activation(egb[:], gb[:], AF.Exp), wait=act.at(), inc=act.inc())
        p.on("vector", lambda e: e.tensor_tensor(gb[:].rearrange("p (a b) -> p a b", b=C),
                                                 gb[:].rearrange("p (a b) -> p a b", b=C),
                                                 gpart[:].unsqueeze(2).broadcast_to([128, NCH, C]), ALU.subtract),
             wait=act.at() + vec.at(), inc=vec.inc())
        for c2 in range(2):
            h2 = slice(c2 * Tf // 2, (c2 + 1) * Tf // 2)
            p.on("vector", lambda e, h2=h2: e.tensor_scalar(dtmp[:], gb[:, h2], 0.0, None, ALU.min),
                 wait=vec.at() + act.at(), inc=vec.inc())
            p.on("scalar", lambda e, h2=h2: e.activation(ltr[:, h2], dtmp[:], AF.Exp), wait=vec.at() + act.at(), inc=act.inc())
            p.on("vector", lambda e, h2=h2: e.tensor_scalar(dtmp[:], gb[:, h2], -1.0, 0.0, ALU.mult, ALU.min),
                 wait=vec.at() + act.at(), inc=vec.inc())
            p.on("scalar", lambda e, h2=h2: e.activation(ltrn[:, h2], dtmp[:], AF.Exp), wait=vec.at() + act.at(), inc=act.inc())
        p.on("vector", lambda e: e.tensor_tensor(LTm[:].rearrange("p (a b) -> p a b", b=C),
                                                 ltr[:].rearrange("p (a b) -> p a b", b=C),
                                                 mLT[:].unsqueeze(1).broadcast_to([128, NCH, C]), ALU.mult),
             wait=act.at() + vec.at() + [(ldg.s, ldgpair)], inc=vec.inc())
        p.on("vector", lambda e: e.tensor_mul(Pmv[:], ltr[:], bb[:]), wait=vec.at(), inc=vec.inc())
        p.on("vector", lambda e: e.tensor_tensor(PTm[:].rearrange("p (a b) -> p a b", b=C),
                                                 Pmv[:].rearrange("p (a b) -> p a b", b=C),
                                                 mPT[:].unsqueeze(1).broadcast_to([128, NCH, C]), ALU.mult),
             wait=vec.at(), inc=vec.inc())
        p.on("vector", lambda e: e.tensor_tensor(ltr[:].rearrange("p (a b) -> p a b", b=C),
                                                 ltrn[:].rearrange("p (a b) -> p a b", b=C),
                                                 bcw[:].unsqueeze(2).broadcast_to([128, NCH, C]), ALU.mult),
             wait=vec.at() + act.at(), inc=vec.inc())
        p.on("vector", lambda e: e.tensor_tensor(Pmv[:].rearrange("p (a b) -> p a b", b=C),
                                                 ltr[:].rearrange("p (a b) -> p a b", b=C),
                                                 mP[:].unsqueeze(1).broadcast_to([128, NCH, C]), ALU.mult),
             wait=vec.at(), inc=vec.inc())
        p.on("vector", lambda e: e.tensor_mul(ltr[:], egb[:], bb[:]), wait=vec.at() + act.at(), inc=vec.inc())
        p.on("vector", lambda e: e.tensor_mul(kbegT[:], kTp[:], ltr[:]), wait=vec.at(), inc=vec.inc())
        p.on("vector", lambda e: e.tensor_mul(qpT[:], qTp[:], egb[:]), wait=vec.at(), inc=vec.inc())
        p.on("vector", lambda e: e.tensor_tensor(vb[:], vp[:], bcw[:].unsqueeze(2).broadcast_to([128, NCH, 64]), ALU.mult),
             wait=vec.at(), inc=vec.inc())
        p.on("vector", lambda e: e.tensor_tensor(ksc[:], kp[:], ftc[:].unsqueeze(2).broadcast_to([128, NCH, 64]), ALU.mult),
             wait=vec.at() + act.at(), inc=vec.inc())
        p.on("vector", lambda e: e.memset(Sf[:], 0.0), wait=vec.at(), inc=vec.inc())
        p.on("vector", lambda e: e.memset(Sbf[:], 0.0), wait=vec.at(), inc=vec.inc())
        pre_done = vec.v

        for ci in range(NCH):
            sl = slice(ci * C, (ci + 1) * C)
            for half in (0, 1):
                o1, o2 = half * 64, half * 64 + 64
                def fS(e, o1=o1, o2=o2, sl=sl):
                    return e.matmul(psS[o1:o2, 0:64], kTp[o1:o2, sl], kTp[o1:o2, sl], start=True, stop=True)
                p.on("tensor", fS, wait=(([(vec.s, pre_done)] + act.at() + vec.at()) if half == 0 else []),
                     inc=([] if half == 0 else mm.inc()))
                def fA(e, o1=o1, o2=o2, sl=sl):
                    return e.matmul(psAT[o1:o2, 0:64], kTp[o1:o2, sl], qTp[o1:o2, sl], start=True, stop=True)
                p.on("tensor", fA, inc=([] if half == 0 else mm.inc()))
                def fW(e, o1=o1, o2=o2, sl=sl):
                    return e.matmul(psW[o1:o2, 0:64], kbegT[o1:o2, sl], Sbf[o1:o2, :], start=True, stop=True)
                p.on("tensor", fW, inc=([] if half == 0 else mm.inc()))
            p.on("vector", lambda e, sl=sl: e.tensor_tensor(PkT[:], psS[:, 0:64], PTm[:, sl], ALU.mult),
                 wait=mm.at() + vec.at(), inc=vec.inc())
            p.on("vector", lambda e, sl=sl: e.tensor_tensor(Pk[:], psS[:, 0:64], Pmv[:, sl], ALU.mult),
                 wait=vec.at(), inc=vec.inc())
            p.on("vector", lambda e, sl=sl: e.tensor_tensor(ATs[:], psAT[:, 0:64], LTm[:, sl], ALU.mult),
                 wait=vec.at(), inc=vec.inc())
            p.on("vector", lambda e, ci=ci: e.tensor_tensor(Usb[0][:], vb[:, ci, :], psW[:, 0:64], ALU.subtract),
                 wait=vec.at(), inc=vec.inc())
            for it in range(6):
                ub = it % 2
                for half in (0, 1):
                    o1, o2 = half * 64, half * 64 + 64
                    def fU(e, o1=o1, o2=o2, ub=ub):
                        return e.matmul(psU[o1:o2, 0:64], PkT[o1:o2, :], Usb[ub][o1:o2, :], start=True, stop=True)
                    p.on("tensor", fU, wait=(vec.at() if half == 0 else []), inc=([] if half == 0 else mm.inc()))
                    if it < 5:
                        def fq1(e, o1=o1, o2=o2):
                            return e.matmul(psQ1[o1:o2, 0:64], PkT[o1:o2, :], Pk[o1:o2, :], start=True, stop=True)
                        p.on("tensor", fq1, inc=([] if half == 0 else mm.inc()))
                        def fq2(e, o1=o1, o2=o2):
                            return e.matmul(psQ2[o1:o2, 0:64], Pk[o1:o2, :], PkT[o1:o2, :], start=True, stop=True)
                        p.on("tensor", fq2, inc=([] if half == 0 else mm.inc()))
                p.on("vector", lambda e, ub=ub: e.tensor_tensor(Usb[1 - ub][:], Usb[ub][:], psU[:, 0:64], ALU.add),
                     wait=mm.at() + vec.at(), inc=vec.inc())
                if it < 5:
                    p.on("vector", lambda e: e.tensor_copy(Pk[:], psQ1[:, 0:64]), wait=vec.at(), inc=vec.inc())
                    p.on("vector", lambda e: e.tensor_copy(PkT[:], psQ2[:, 0:64]), wait=vec.at(), inc=vec.inc())
            vnew = Usb[0]
            for half in (0, 1):
                o1, o2 = half * 64, half * 64 + 64
                def fo1(e, o1=o1, o2=o2, sl=sl):
                    return e.matmul(psO[o1:o2, 0:64], qpT[o1:o2, sl], Sbf[o1:o2, :], start=True, stop=False)
                p.on("tensor", fo1, wait=(vec.at() if half == 0 else []))
                def fo2(e, o1=o1, o2=o2):
                    return e.matmul(psO[o1:o2, 0:64], ATs[o1:o2, :], vnew[o1:o2, :], start=False, stop=True)
                p.on("tensor", fo2, inc=([] if half == 0 else mm.inc()))
                def fsu(e, o1=o1, o2=o2, ci=ci):
                    return e.matmul(psSu[o1:o2, 0:64], ksc[o1:o2, ci, :], vnew[o1:o2, :], start=True, stop=True)
                p.on("tensor", fsu, inc=([] if half == 0 else mm.inc()))
            p.on("vector", lambda e, ci=ci: e.tensor_copy(op_[:, ci, :], psO[:, 0:64]),
                 wait=mm.at() + vec.at() + ((sto.at(0) + sto.at(1)) if ci == 0 and hp > 0 else []), inc=vec.inc())
            p.on("vector", lambda e, ci=ci: e.scalar_tensor_tensor(Sf[:], Sf[:], egl[:, ci:ci + 1], psSu[:, 0:64],
                                                                   ALU.mult, ALU.add),
                 wait=vec.at(), inc=vec.inc())
            p.on("vector", lambda e: e.tensor_copy(Sbf[:], Sf[:]), wait=vec.at(), inc=vec.inc())
        for half in (0, 1):
            def fst(e, hp=hp, half=half):
                return e.dma_start(
                    out=od.rearrange("(ch c) d -> c ch d", c=C)[:, :, hp * 128 + half * 64: hp * 128 + (half + 1) * 64],
                    in_=op_[half * 64:(half + 1) * 64, :, :])
            p.on("sync", fst, wait=vec.at(), inc=sto.inc(hp * 2 + half))
    p.on("sync", lambda e: e.nop(), wait=sto.at(0) + sto.at(1))
    return p.build()


def build_l3_oepi(name="l3oepi"):
    p = Prog(name)
    S = 4096
    od = p.inp("o", [S, 512], BF16)
    gd = p.inp("gate", [S, 512], BF16)
    nwd = p.inp("nw", [128, 512])
    ood = p.outp("oo", [S, 512], BF16)
    ld = DmaCtr(p, "ld", 2)
    ldc = Ctr(p, "ldc")
    vec = Ctr(p, "vec")
    act = Ctr(p, "act")
    sto = DmaCtr(p, "sto", 2)
    nwb = p.sb("nw", [128, 512])
    p.on("sync", lambda e: e.dma_start(out=nwb[:], in_=nwd[:]), inc=ldc.inc(16))
    ot = [p.sb("ot", [128, 512], BF16) for _ in range(2)]
    gt = [p.sb("gt", [128, 512], BF16) for _ in range(2)]
    x2 = p.sb("x2", [128, 512])
    ssum = p.sb("ssum", [128, 8])
    rn = p.sb("rn", [128, 8])
    t1 = p.sb("t1", [128, 512])
    sg = p.sb("sg", [128, 512])
    oo = [p.sb("oo", [128, 512], BF16) for _ in range(2)]
    for i in range(S // 128):
        b = i % 2
        w = sto.at(b) if i >= 2 else []
        p.on("sync", lambda e, i=i, b=b: e.dma_start(out=ot[b][:], in_=od[i * 128:(i + 1) * 128, :]), wait=w, inc=ld.inc(b))
        p.on("sync", lambda e, i=i, b=b: e.dma_start(out=gt[b][:], in_=gd[i * 128:(i + 1) * 128, :]), inc=ld.inc(b))
        p.on("scalar", lambda e, b=b: e.activation(x2[:], ot[b][:], AF.Square),
             wait=ld.at(b) + act.at() + vec.at() + (ldc.at() if i == 0 else []), inc=act.inc())
        p.on("vector", lambda e: e.reduce_sum(ssum[:], x2[:].rearrange("p (h k) -> p h k", k=64), axis=AX.X),
             wait=act.at() + vec.at(), inc=vec.inc())
        p.on("vector", lambda e: e.tensor_scalar(ssum[:], ssum[:], 1.0 / 64.0, 1e-5, ALU.mult, ALU.add),
             wait=vec.at(), inc=vec.inc())
        p.on("scalar", lambda e: e.activation(ssum[:], ssum[:], AF.Sqrt), wait=vec.at() + act.at(), inc=act.inc())
        p.on("vector", lambda e: e.reciprocal(rn[:], ssum[:]), wait=act.at() + vec.at(), inc=vec.inc())
        p.on("vector", lambda e, b=b: e.tensor_tensor(t1[:].rearrange("p (h k) -> p h k", k=64),
                                                      ot[b][:].rearrange("p (h k) -> p h k", k=64),
                                                      rn[:].unsqueeze(2).broadcast_to([128, 8, 64]), ALU.mult),
             wait=vec.at(), inc=vec.inc())
        p.on("vector", lambda e: e.tensor_mul(t1[:], t1[:], nwb[:]), wait=vec.at(), inc=vec.inc())
        p.on("scalar", lambda e, b=b: e.activation(sg[:], gt[b][:], AF.Silu), wait=act.at() + vec.at(), inc=act.inc())
        p.on("vector", lambda e, b=b: e.tensor_mul(oo[b][:], t1[:], sg[:]), wait=act.at() + vec.at(), inc=vec.inc())
        p.on("sync", lambda e, i=i, b=b: e.dma_start(out=ood[i * 128:(i + 1) * 128, :], in_=oo[b][:]),
             wait=vec.at(), inc=sto.inc(b))
    p.on("sync", lambda e: e.nop(), wait=sto.at(0) + sto.at(1))
    return p.build()


def dnet_host_inputs(qn, kn, v, beta, decay):
    S = 4096
    def cw(a, dt):
        out = np.empty((4, 128, 64, 64), dtype=dt)
        ar = np.asarray(a)
        for hp in range(4):
            for sub in range(2):
                h = 2 * hp + sub
                blk = ar[:, h * 64:(h + 1) * 64].reshape(64, 64, 64)
                out[hp, sub * 64:(sub + 1) * 64] = blk.transpose(1, 0, 2)
        return out
    def cw8(a, dt):
        out = np.empty((4, 128, 64), dtype=dt)
        ar = np.asarray(a)
        for hp in range(4):
            for sub in range(2):
                h = 2 * hp + sub
                out[hp, sub * 64:(sub + 1) * 64] = ar[:, h].reshape(64, 64).T
        return out
    pm = np.arange(128) % 64
    cc = np.arange(64)
    ins = {
        "qT": np.ascontiguousarray(np.asarray(qn).T),
        "kT": np.ascontiguousarray(np.asarray(kn).T),
        "v_cw": cw(v, BF), "k_cw": cw(kn, BF),
        "beta_cw": cw8(beta, np.float32), "decay_cw": cw8(decay, np.float32),
        "decayT": np.ascontiguousarray(decay.T), "betaT": np.ascontiguousarray(beta.T),
        "trilI2": (pm[:, None] <= cc[None, :]).astype(np.float32),
        "mLT": (pm[:, None] <= cc[None, :]).astype(BF),
        "mPT": (pm[:, None] < cc[None, :]).astype(BF),
        "mP": (pm[:, None] > cc[None, :]).astype(BF),
    }
    return ins


def build_amax3(shapes, name="amax3"):
    p = Prog(name)
    ld = Ctr(p, "ld")
    vec = Ctr(p, "vec")
    sto = Ctr(p, "sto")
    for idx, (R, C) in enumerate(shapes):
        wd = p.inp(f"w{idx}", [R, C])
        od = p.outp(f"am{idx}", [1, 1])
        wt = p.sb(f"wt{idx}", [128, C])
        pm = p.sb(f"pm{idx}", [128, 1])
        res = p.sb(f"res{idx}", [1, 1])
        fl = p.sb(f"fl{idx}", [1, R])
        colsum = p.dram(f"cs{idx}", [R // 128, 128])
        for ri in range(R // 128):
            p.on("sync", lambda e, ri=ri, wd=wd, wt=wt: e.dma_start(out=wt[:], in_=wd[ri * 128:(ri + 1) * 128, :]),
                 wait=(vec.at() if ri >= 1 else []), inc=ld.inc(16))
            p.on("vector", lambda e, wt=wt, pm=pm: e.reduce_max(pm[:], wt[:], axis=AX.X, apply_absolute_value=True),
                 wait=ld.at() + vec.at(), inc=vec.inc())
            p.on("sync", lambda e, ri=ri, colsum=colsum, pm=pm: e.dma_start(
                out=colsum[ri:ri + 1, :].rearrange("a b -> b a"), in_=pm[:]),
                wait=vec.at(), inc=ld.inc(16))
        p.on("sync", lambda e, fl=fl, colsum=colsum: e.dma_start(out=fl[:], in_=colsum.rearrange("a b -> (a b)").unsqueeze(0)),
             wait=ld.at(), inc=ld.inc(16))
        p.on("vector", lambda e, res=res, fl=fl: e.reduce_max(res[:], fl[:], axis=AX.X),
             wait=ld.at() + vec.at(), inc=vec.inc())
        p.on("sync", lambda e, od=od, res=res: e.dma_start(out=od[:], in_=res[:]), wait=vec.at(), inc=sto.inc(16))
    p.on("sync", lambda e: e.nop(), wait=sto.at())
    return p.build()


def build_quant_l2w(R, C, cw=512, name="quantl2w"):
    p = Prog(name)
    wd = p.inp("w", [R, C])
    amd = p.inp("amax8", [1, 8])
    od = p.outp("wq", [R, C], BF16)
    gscr = p.dram("gs_scr", [1, 1])
    ld = Ctr(p, "ld")
    vec = Ctr(p, "vec")
    act = Ctr(p, "act")
    sto = DmaCtr(p, "sto", 2)
    am = p.sb("am", [1, 8])
    gm = p.sb("gm", [1, 1])
    gsb = p.sb("gs", [128, 1])
    gs6 = p.sb("gs6", [128, 1])
    rgs = p.sb("rgs", [128, 1])
    p.on("sync", lambda e: e.dma_start(out=am[:], in_=amd[:]), inc=ld.inc(16))
    p.on("vector", lambda e: e.reduce_max(gm[:], am[:], axis=AX.X), wait=ld.at(), inc=vec.inc())
    p.on("vector", lambda e: e.reciprocal(gm[:], gm[:]), wait=vec.at(), inc=vec.inc())
    p.on("vector", lambda e: e.tensor_scalar(gm[:], gm[:], 2688.0, None, ALU.mult), wait=vec.at(), inc=vec.inc())
    p.on("sync", lambda e: e.dma_start(out=gscr[:], in_=gm[:]), wait=vec.at(), inc=ld.inc(16))
    p.on("sync", lambda e: e.dma_start(out=gsb[:], in_=gscr[0].unsqueeze(0).broadcast_to([128, 1])), inc=ld.inc(16))
    p.on("vector", lambda e: e.tensor_scalar(gs6[:], gsb[:], 1.0 / 6.0, None, ALU.mult), wait=ld.at() + vec.at(), inc=vec.inc())
    p.on("vector", lambda e: e.reciprocal(rgs[:], gsb[:]), wait=ld.at() + vec.at(), inc=vec.inc())
    consts = dict(gs6=gs6, gs=gsb, rgs=rgs)
    sc = make_quant_scratch(p, cw)
    wt = [p.sb("wt", [128, cw]) for _ in range(2)]
    oq = [p.sb("oq", [128, cw], BF16) for _ in range(2)]
    k = 0
    for ri in range(R // 128):
        for ci in range(C // cw):
            b = k % 2
            wld = (vec.at() + act.at()) if k >= 2 else []
            p.on("sync", lambda e, ri=ri, ci=ci, b=b: e.dma_start(
                out=wt[b][:], in_=wd[ri * 128:(ri + 1) * 128, ci * cw:(ci + 1) * cw]),
                wait=wld, inc=ld.inc(16))
            emit_quant(p, sc, wt[b][:], oq[b][:], "l2", vec, act, consts,
                       wait0=ld.at() + (sto.at(b) if k >= 2 else []))
            p.on("sync", lambda e, ri=ri, ci=ci, b=b: e.dma_start(
                out=od[ri * 128:(ri + 1) * 128, ci * cw:(ci + 1) * cw], in_=oq[b][:]),
                wait=vec.at(), inc=sto.inc(b))
            k += 1
    p.on("sync", lambda e: e.nop(), wait=sto.at(0) + sto.at(1))
    return p.build()


CAPTURE = None


def _c(a):
    return np.ascontiguousarray(a)


def kernel(**inputs):
    F = np.float32
    x0 = np.asarray(inputs["L0_attn_hidden_states"], F)
    X = [_c(own_rows(x0[c // 2], c % 2)) for c in range(8)]
    idn = np.eye(128, dtype=F)
    masks = [attn_masks(c % 2) for c in range(8)]

    def run(key, builder, ins):
        return runp(cached_nc(key, builder), ins)

    wgu = np.concatenate([np.asarray(inputs["L0_ffn_w_gate"], F), np.asarray(inputs["L0_ffn_w_up"], F)], 0)
    wgup = np.concatenate([wgu, np.zeros((512, 1024), F)], 0)
    r = run("q_l0gu", lambda: build_quant(768, 1024, "l0"),
            [{"w": _c(wgup[c * 768:(c + 1) * 768])} for c in range(8)])
    wq_gu = np.concatenate([np.asarray(rr["wq"]) for rr in r], 0)[:5632]
    wqg, wqu = wq_gu[:2816], wq_gu[2816:]
    wdn = np.asarray(inputs["L0_ffn_w_down"], F)
    r = run("q_l0d", lambda: build_quant(128, 2816, "l0", cw=256),
            [{"w": _c(wdn[c * 128:(c + 1) * 128])} for c in range(8)])
    wqd = np.concatenate([np.asarray(rr["wq"]) for rr in r], 0)
    gw2 = np.asarray(inputs["L2_ffn_gate_w"], F)
    uw2 = np.asarray(inputs["L2_ffn_up_w"], F)
    dw2 = np.asarray(inputs["L2_ffn_down_w"], F)
    r = run("amax3", lambda: build_amax3([(512, 1024), (512, 1024), (128, 4096)]),
            [{"w0": _c(gw2[c * 512:(c + 1) * 512]), "w1": _c(uw2[c * 512:(c + 1) * 512]),
              "w2": _c(dw2[c * 128:(c + 1) * 128])} for c in range(8)])
    am_g = np.concatenate([np.asarray(rr["am0"]) for rr in r], 1)
    am_u = np.concatenate([np.asarray(rr["am1"]) for rr in r], 1)
    am_d = np.concatenate([np.asarray(rr["am2"]) for rr in r], 1)
    r = run("q_l2g", lambda: build_quant_l2w(512, 1024),
            [{"w": _c(gw2[c * 512:(c + 1) * 512]), "amax8": am_g} for c in range(8)])
    wq2g = np.concatenate([np.asarray(rr["wq"]) for rr in r], 0)
    r = run("q_l2u", lambda: build_quant_l2w(512, 1024),
            [{"w": _c(uw2[c * 512:(c + 1) * 512]), "amax8": am_u} for c in range(8)])
    wq2u = np.concatenate([np.asarray(rr["wq"]) for rr in r], 0)
    r = run("q_l2d", lambda: build_quant_l2w(128, 4096),
            [{"w": _c(dw2[c * 128:(c + 1) * 128]), "amax8": am_d} for c in range(8)])
    wq2d = np.concatenate([np.asarray(rr["wq"]) for rr in r], 0)

    def ln_all():
        r = run("ln", build_ln, [{"x": X[c]} for c in range(8)])
        return [np.asarray(rr["xn"]) for rr in r]

    def combine_all(slist, glist, sdt=BF16):
        key = "comb_bf" if sdt == BF16 else "comb_f32"
        r = run(key, lambda: build_combine(sdt=sdt),
                [{"x": X[c], "s": _c(slist[c]), "gr": _c(glist[c])} for c in range(8)])
        for c in range(8):
            X[c] = np.asarray(r[c]["xn2"])
        if CAPTURE is not None:
            CAPTURE.append([X[c].copy() for c in range(8)])

    xn = ln_all()
    cak = np.asarray(inputs["L0_attn_c_attn_kernel"], F)
    cab = np.asarray(inputs["L0_attn_c_attn_bias"], F)
    r = run("l0qkv", lambda: build_proj(2048, 1024, [
        dict(name="qT", cols=1024, layout="fT", bias=True, scale=0.125),
        dict(name="kT", cols=1024, layout="fT", bias=True),
        dict(name="v", cols=1024, layout="tok", bias=True)], name="l0qkv"),
        [{"xnT": _c(xn[c].T), "W_qT": _c(cak[0:1024].T), "b_qT": _c(cab[0:1024].reshape(8, 128).T),
          "W_kT": _c(cak[1024:2048].T), "b_kT": _c(cab[1024:2048].reshape(8, 128).T),
          "W_v": _c(cak[2048:3072].T), "b_v": _c(cab[2048:3072].reshape(1, 1024))} for c in range(8)])
    qT = [np.asarray(rr["o_qT"]) for rr in r]
    kT = [np.asarray(rr["o_kT"]) for rr in r]
    vv = [np.asarray(rr["o_v"]) for rr in r]
    ins = []
    for c in range(8):
        b, f = c // 2, c % 2
        kTf = interleave_cols(kT[2 * b], kT[2 * b + 1])
        vfull = interleave_halves(vv[2 * b], vv[2 * b + 1])
        va = np.ones((4096, 16 * 65), dtype=BF)
        for h in range(16):
            va[:, h * 65:h * 65 + 64] = vfull[:, h * 64:(h + 1) * 64]
        ins.append({"qT": qT[c], "kT": _c(kTf), "vv": va, "masks": masks[c], "idn": idn,
                    "WpT": _c(np.asarray(inputs["L0_attn_c_proj_kernel"], F).T),
                    "bp": _c(np.asarray(inputs["L0_attn_c_proj_bias"], F).reshape(8, 128).T)})
    r = run("l0attn", lambda: build_attn(16, 64, True, "attn0"), ins)
    sT = [np.asarray(rr["sT"]) for rr in r]
    r = run("q_xn0", lambda: build_quant(2048, 1024, "l0", name="qxn0"), [{"w": xn[c]} for c in range(8)])
    xq = [np.asarray(rr["wq"]) for rr in r]
    r = run("l0gu", lambda: build_ffn_gu(2816, "l0", name="l0gu"),
            [{"xqT": _c(xq[c].T), "wgT": _c(wqg.T), "wuT": _c(wqu.T)} for c in range(8)])
    hq = [np.asarray(rr["hq"]) for rr in r]
    r = run("l0down", lambda: build_proj(2048, 2816, [dict(name="gff", cols=1024, layout="tok", wdt=BF16)],
                                         in_dtype=BF16, name="l0down"),
            [{"xnT": _c(hq[c].T), "W_gff": _c(wqd.T)} for c in range(8)])
    gff = [np.asarray(rr["o_gff"]) for rr in r]
    combine_all([sT[c].T for c in range(8)], gff)

    xn = ln_all()
    r = run("l1zh", lambda: build_proj(2048, 1024, [
        dict(name="zT", cols=1024, layout="fT", bias=True, act=AF.Sigmoid, odt=F32),
        dict(name="hT", cols=1024, layout="fT", bias=True, odt=F32)], name="l1zh"),
        [{"xnT": _c(xn[c].T), "W_zT": np.asarray(inputs["L1_attn_Wz"], F),
          "b_zT": _c(np.asarray(inputs["L1_attn_bz"], F).reshape(8, 128).T),
          "W_hT": np.asarray(inputs["L1_attn_Wh"], F),
          "b_hT": _c(np.asarray(inputs["L1_attn_bh"], F).reshape(8, 128).T)} for c in range(8)])
    zT = [np.asarray(rr["o_zT"]) for rr in r]
    hTT = [np.asarray(rr["o_hT"]) for rr in r]
    SC, BU, BD = l1_shift_consts()
    logDc = _c(np.asarray(inputs["L1_attn_log_D"], F).reshape(8, 128).T)
    ins = []
    for c in range(8):
        b = c // 2
        ins.append({"zT": _c(interleave_cols(zT[2 * b], zT[2 * b + 1])),
                    "htT": _c(interleave_cols(hTT[2 * b], hTT[2 * b + 1])),
                    "logD": logDc, "SC": SC, "BU": BU, "BD": BD})
    r = run("l1scan", build_l1_scan, ins)
    hdT = [np.asarray(rr["hdT"]) for rr in r]
    r = run("l1ffn", build_l1_ffn,
            [{"xnT": _c(xn[c].T), "U1": np.asarray(inputs["L1_ffn_U1"], F),
              "V1": np.asarray(inputs["L1_ffn_V1"], F),
              "b1c": _c(np.asarray(inputs["L1_ffn_b1"], F).reshape(32, 128).T),
              "U2": np.asarray(inputs["L1_ffn_U2"], F), "V2": np.asarray(inputs["L1_ffn_V2"], F),
              "b2": _c(np.asarray(inputs["L1_ffn_b2"], F).reshape(1, 1024))} for c in range(8)])
    gff = [np.asarray(rr["gff"]) for rr in r]
    combine_all([own_rows(hdT[c].T, c % 2) for c in range(8)], gff, sdt=F32)

    xn = ln_all()
    r = run("l2qkv", lambda: build_proj(2048, 1024, [
        dict(name="qT", cols=1024, layout="fT", scale=1.0 / 32.0),
        dict(name="kT", cols=1024, layout="fT"),
        dict(name="v", cols=1024, layout="tok")], name="l2qkv"),
        [{"xnT": _c(xn[c].T), "W_qT": np.asarray(inputs["L2_attn_WQ"], F),
          "W_kT": np.asarray(inputs["L2_attn_WK"], F),
          "W_v": np.asarray(inputs["L2_attn_WV"], F)} for c in range(8)])
    qT = [np.asarray(rr["o_qT"]) for rr in r]
    kT = [np.asarray(rr["o_kT"]) for rr in r]
    vv = [np.asarray(rr["o_v"]) for rr in r]
    ins = []
    for c in range(8):
        b = c // 2
        ins.append({"qT": qT[c], "kT": _c(interleave_cols(kT[2 * b], kT[2 * b + 1])),
                    "vv": _c(interleave_halves(vv[2 * b], vv[2 * b + 1])),
                    "masks": masks[c], "idn": idn})
    r = run("l2attn", lambda: build_attn(1, 1024, False, "attn2"), ins)
    sT = [np.asarray(rr["sT"]) for rr in r]
    gs_in = np.broadcast_to(np.asarray(inputs["L2_ffn_s_in"], F).reshape(1, 1), (128, 1))
    gs_dn = np.broadcast_to(np.asarray(inputs["L2_ffn_s_in_down"], F).reshape(1, 1), (128, 1))
    r = run("q_xn2", lambda: build_quant(2048, 1024, "l2", name="qxn2"),
            [{"w": xn[c], "gs": _c(gs_in)} for c in range(8)])
    xq = [np.asarray(rr["wq"]) for rr in r]
    r = run("l2gu", lambda: build_ffn_gu(4096, "l2", name="l2gu"),
            [{"xqT": _c(xq[c].T), "wgT": _c(wq2g.T), "wuT": _c(wq2u.T), "gs": _c(gs_dn)} for c in range(8)])
    hq = [np.asarray(rr["hq"]) for rr in r]
    r = run("l2down", lambda: build_proj(2048, 4096, [dict(name="gff", cols=1024, layout="tok", wdt=BF16)],
                                         in_dtype=BF16, name="l2down"),
            [{"xnT": _c(hq[c].T), "W_gff": _c(wq2d.T)} for c in range(8)])
    gff = [np.asarray(rr["o_gff"]) for rr in r]
    combine_all([sT[c].T for c in range(8)], gff)

    xn = ln_all()
    Wq3 = np.asarray(inputs["L3_attn_Wq"], F)
    Wk3 = np.asarray(inputs["L3_attn_Wk"], F)
    Wv3 = np.asarray(inputs["L3_attn_Wv"], F)
    Wg3 = np.asarray(inputs["L3_attn_Wg"], F)
    Wb3 = np.asarray(inputs["L3_attn_Wb"], F)
    Wa3 = np.asarray(inputs["L3_attn_Wa"], F)
    dtb = np.asarray(inputs["L3_attn_dt_bias"], F)
    ins = []
    xnT_full = {}
    for c in range(8):
        b, f = c // 2, c % 2
        if b not in xnT_full:
            xnT_full[b] = _c(interleave_halves(xn[2 * b], xn[2 * b + 1]).T)
        sl = slice(512 * f, 512 * (f + 1))
        sl8 = slice(8 * f, 8 * (f + 1))
        ins.append({"xnT": xnT_full[b], "W_q": _c(Wq3[:, sl]), "W_k": _c(Wk3[:, sl]),
                    "W_v": _c(Wv3[:, sl]), "W_gate": _c(Wg3[:, sl]),
                    "W_beta": _c(Wb3[:, sl8]), "W_rawa": _c(Wa3[:, sl8]),
                    "b_rawa": _c(dtb[sl8].reshape(1, 8))})
    r = run("l3proj", lambda: build_proj(4096, 1024, [
        dict(name="q", cols=512, layout="tok", odt=F32),
        dict(name="k", cols=512, layout="tok", odt=F32),
        dict(name="v", cols=512, layout="tok"),
        dict(name="gate", cols=512, layout="tok"),
        dict(name="beta", cols=8, layout="tok", act=AF.Sigmoid, odt=F32),
        dict(name="rawa", cols=8, layout="tok", bias=True, odt=F32)], name="l3proj"), ins)
    q3 = [np.asarray(rr["o_q"]) for rr in r]
    k3 = [np.asarray(rr["o_k"]) for rr in r]
    v3 = [np.asarray(rr["o_v"]) for rr in r]
    g3 = [np.asarray(rr["o_gate"]) for rr in r]
    beta3 = [np.asarray(rr["o_beta"]) for rr in r]
    rawa3 = [np.asarray(rr["o_rawa"]) for rr in r]
    Alg = np.asarray(inputs["L3_attn_A_log"], F)
    r = run("l3prep", build_l3_prep,
            [{"q": q3[c], "k": k3[c], "rawa": rawa3[c],
              "A_log": _c(np.broadcast_to(Alg[8 * (c % 2):8 * (c % 2) + 8], (128, 8)))} for c in range(8)])
    qn3 = [np.asarray(rr["qn"]) for rr in r]
    kn3 = [np.asarray(rr["kn"]) for rr in r]
    dec3 = [np.asarray(rr["decay"]) for rr in r]
    r = run("l3gate", build_l3_gate,
            [{"xnT": _c(xn[c].T), "gwT": _c(np.asarray(inputs["L3_ffn_gate_w"], F).T),
              "lb": _c(np.asarray(inputs["L3_ffn_lb_bias"], F).reshape(1, 8))} for c in range(8)])
    selw = [np.asarray(rr["selw"]) for rr in r]
    parts = np.stack([np.asarray(rr["partials"]) for rr in r]).reshape(8, 16)
    r = run("dnet", build_deltanet,
            [dnet_host_inputs(qn3[c], kn3[c], v3[c], beta3[c], dec3[c]) for c in range(8)])
    o3 = [np.asarray(rr["o"]) for rr in r]
    nw = _c(np.broadcast_to(np.tile(np.asarray(inputs["L3_attn_norm_w"], F), 8), (128, 512)))
    r = run("l3oepi", build_l3_oepi,
            [{"o": o3[c], "gate": g3[c], "nw": nw} for c in range(8)])
    oo3 = [np.asarray(rr["oo"]) for rr in r]
    ins = []
    for c in range(8):
        b, f = c // 2, c % 2
        ofull = np.concatenate([oo3[2 * b], oo3[2 * b + 1]], 1)
        oown = own_rows(ofull, f)
        ins.append({"xnT": _c(oown.T), "W_sT": np.asarray(inputs["L3_attn_Wo"], F)})
    r = run("l3out", lambda: build_proj(2048, 1024, [dict(name="sT", cols=1024, layout="fT")],
                                        in_dtype=BF16, name="l3out"), ins)
    sT = [np.asarray(rr["o_sT"]) for rr in r]
    egw = np.asarray(inputs["L3_ffn_expert_gate_w"], F)
    euw = np.asarray(inputs["L3_ffn_expert_up_w"], F)
    edw = np.asarray(inputs["L3_ffn_expert_down_w"], F)
    sgw = np.asarray(inputs["L3_ffn_shared_gate_w"], F)
    suw = np.asarray(inputs["L3_ffn_shared_up_w"], F)
    sdw = np.asarray(inputs["L3_ffn_shared_down_w"], F)
    egwT = _c(np.concatenate([egw[e].T for e in range(8)] + [sgw.T[:, :512], sgw.T[:, 512:]], 1))
    euwT = _c(np.concatenate([euw[e].T for e in range(8)] + [suw.T[:, :512], suw.T[:, 512:]], 1))
    edwR = _c(np.concatenate([edw[e].T for e in range(8)] + [sdw.T[:512], sdw.T[512:]], 0))
    r = run("moe", build_moe,
            [{"xnT": _c(xn[c].T), "selw": selw[c], "partials": parts,
              "egwT": egwT, "euwT": euwT, "edwR": edwR} for c in range(8)])
    gmoe = [np.asarray(rr["gmoe"]) for rr in r]
    combine_all([sT[c].T for c in range(8)], gmoe)

    out = np.empty((4, 4096, 1024), F)
    for b in range(4):
        out[b] = interleave_halves(X[2 * b], X[2 * b + 1])
    return out
